# version 23
# baseline (speedup 1.0000x reference)
import numpy as np
import ml_dtypes
import concourse.bass as bass
import concourse.mybir as mybir
from concourse.bass_utils import run_bass_kernel_spmd
from contextlib import ExitStack

F32 = mybir.dt.float32
BF16 = mybir.dt.bfloat16
FP16 = mybir.dt.float16
FP8 = mybir.dt.float8e4
AF = mybir.ActivationFunctionType
ALU = mybir.AluOpType

S = 2048
H = 1024
NH = 16
NT = 16
NF = 8
SCALE = 64.0 ** -0.5
EPS = 1e-5

ENGS = ("tensor", "vector", "scalar", "sync", "gpsimd")


class Sem:
    def __init__(self, handle):
        self.h = handle
        self.count = 0


class Prog:

    def __init__(self, nc, ctx):
        self.nc = nc
        self.ctx = ctx
        self.streams = {e: [] for e in ENGS}
        self.tags = {e: [] for e in ENGS}
        self.sems = {}
        self._last_wait = {}

    def sem(self, name):
        if name not in self.sems:
            self.sems[name] = Sem(self.ctx.enter_context(self.nc.semaphore(name)))
        return self.sems[name]

    def emit(self, eng, fn, waits=(), incs=()):
        import sys as _sys
        wl = []
        for s, v in waits:
            key = (eng, s.h.name)
            if v > 0 and self._last_wait.get(key, -1) < v:
                self._last_wait[key] = v
                wl.append((s, v))
        for s, k in incs:
            s.count += k
        self.streams[eng].append((fn, wl, list(incs)))
        self.tags[eng].append(_sys._getframe(1).f_lineno)

    def finalize(self):
        block = self.ctx.enter_context(self.nc.Block())
        streams = self.streams

        def runner(name):
            def _run(eng):
                with self.nc.allow_low_precision(
                        reason="intentional bf16 stores; accumulation stays fp32"):
                    for fn, waits, incs in streams[name]:
                        for s, v in waits:
                            eng.wait_ge(s.h, v)
                        inst = fn(eng)
                        for s, k in incs:
                            inst.then_inc(s.h, k)
            return _run

        block.tensor(runner("tensor"))
        block.vector(runner("vector"))
        block.scalar(runner("scalar"))
        block.sync(runner("sync"))
        block.gpsimd(runner("gpsimd"))
        self.nc.op_tags = self.tags


def _tri_masks():
    tl = np.arange(128)[:, None]
    sl = np.arange(128)[None, :]
    tri_hi = (tl <= sl).astype(np.float32)
    tri_lo = (tl >= sl).astype(np.float32)
    tri_lo_r0 = np.maximum(tri_lo, np.broadcast_to((tl == 0), (128, 128)).astype(np.float32))
    tri_hi_c0 = np.maximum(tri_hi, np.broadcast_to((sl == 0), (128, 128)).astype(np.float32))
    m = np.zeros((128, 1024), np.float32)
    m[:, 0:128] = tri_hi
    m[:, 128:256] = tri_lo
    m[:, 256:384] = tri_hi_c0
    m[:, 384:512] = tri_lo
    m[:, 512:640] = tri_lo_r0
    m[:, 640:768] = tri_hi
    return m.astype(ml_dtypes.bfloat16)


def build(debug=False, phase_limit=None, light=False):
    nc = bass.Bass("TRN2", target_bir_lowering=False)

    x_d = nc.dram_tensor("x", [S, H], F32, kind="ExternalInput").ap()
    wqT_d = nc.dram_tensor("wqT", [H, H], F32, kind="ExternalInput").ap()
    wkT_d = nc.dram_tensor("wkT", [H, H], F32, kind="ExternalInput").ap()
    wvT_d = nc.dram_tensor("wvT", [H, H], F32, kind="ExternalInput").ap()
    woT_d = nc.dram_tensor("woT", [H, H], F32, kind="ExternalInput").ap()
    wfT_d = nc.dram_tensor("wfT", [H, H], F32, kind="ExternalInput").ap()
    biasr_d = nc.dram_tensor("biasr", [128, H], F32, kind="ExternalInput").ap()
    masks_d = nc.dram_tensor("masks", [128, 1024], BF16, kind="ExternalInput").ap()
    idbf_d = nc.dram_tensor("idbf", [128, 128], BF16, kind="ExternalInput").ap()
    id8_d = nc.dram_tensor("id8", [128, 128], FP8, kind="ExternalInput").ap()
    idf16_d = nc.dram_tensor("idf16", [128, 128], FP16, kind="ExternalInput").ap()
    onesbf_d = nc.dram_tensor("onesbf", [128, 128], BF16, kind="ExternalInput").ap()
    y_d = nc.dram_tensor("y", [S, H], F32, kind="ExternalOutput").ap()
    dsprd = nc.dram_tensor("dsprd", [2, 2048], BF16, kind="ExternalOutput").ap()

    dbg = {}
    if light:
        for name, shape, dt in [("d_u", [128, S], BF16), ("d_rc2", [128, S], BF16),
                                ("d_attnT", [128, NF * S], BF16)]:
            dbg[name] = nc.dram_tensor(name, shape, dt, kind="ExternalOutput").ap()
    if debug:
        for name, shape, dt in [
            ("d_xnT", [128, NF * S], BF16), ("d_xqT", [128, NF * S], FP8),
            ("d_qT", [128, NF * S], BF16), ("d_kT", [128, NF * S], BF16),
            ("d_u", [128, S], BF16), ("d_rc2", [128, S], BF16),
            ("d_v", [128, NT * 1040], BF16), ("d_attnT", [128, NF * S], BF16),
            ("d_ffn", [128, NF * S], FP16), ("d_e0", [97, 4 * S], BF16),
            ("d_e0q", [128, 256], BF16), ("d_stats", [128, 6 * NT], F32),
        ]:
            dbg[name] = nc.dram_tensor(name, shape, dt, kind="ExternalOutput").ap()

    with ExitStack() as ctx:
        e = ctx.enter_context
        xnT = e(nc.sbuf_tensor("xnT_attnT", [128, NF * S], BF16))
        attnT = xnT
        qtA = e(nc.sbuf_tensor("qt_ffn", [128, NF * S], BF16))
        QT = qtA
        ffn16 = qtA.bitcast(FP16)
        ktA = e(nc.sbuf_tensor("kt_xre", [128, NF * S], BF16))
        KT = ktA
        x_re = ktA.bitcast(F32)
        vA = e(nc.sbuf_tensor("v_out", [128, NT * 1040], BF16))
        V = vA
        out_f32 = vA.bitcast(F32)
        wbfA = e(nc.sbuf_tensor("wbf_a", [128, NF * H], BF16))
        E0 = wbfA
        wbfB = e(nc.sbuf_tensor("wbf_b", [128, NF * H], BF16))
        wf8 = e(nc.sbuf_tensor("wf8", [128, NF * H], FP8))
        xqT = e(nc.sbuf_tensor("xqT", [128, NF * S], FP8))
        otbf = xqT.bitcast(BF16)

        masks = e(nc.sbuf_tensor("sb_masks", [128, 640], BF16))
        idbf = e(nc.sbuf_tensor("sb_idbf", [128, 128], BF16))
        id8 = e(nc.sbuf_tensor("sb_id8", [128, 128], FP8))
        idf16 = e(nc.sbuf_tensor("sb_idf16", [128, 128], FP16))
        onesbf = e(nc.sbuf_tensor("sb_onesbf", [128, 128], BF16))
        U = e(nc.sbuf_tensor("U", [128, S], BF16))
        Esb = e(nc.sbuf_tensor("Esb", [128, 4 * 896], BF16))
        scA = e(nc.sbuf_tensor("scA", [128, S], BF16))
        rc2 = e(nc.sbuf_tensor("rc2", [128, S], BF16))
        W01 = wbfA
        zb = rc2[:, 0:512]
        E0sp = e(nc.sbuf_tensor("E0sp", [128, 16 * 16], BF16))
        V0c = e(nc.sbuf_tensor("V0c", [128, 128], BF16))
        W1c = e(nc.sbuf_tensor("W1c", [128, 128], BF16))
        Gsp = e(nc.sbuf_tensor("Gsp", [128, 16], BF16))
        Odsp = e(nc.sbuf_tensor("Odsp", [1, 256], BF16))
        Dspr = e(nc.sbuf_tensor("Dspr", [128, 32], BF16))
        Dsp = id8.bitcast(F32)
        sumc = e(nc.sbuf_tensor("sumc", [128, NT], F32))
        cinvH = e(nc.sbuf_tensor("cinvH", [128, 4], F32))
        cnegH = e(nc.sbuf_tensor("cnegH", [128, 4], F32))
        cepsm = e(nc.sbuf_tensor("cepsm", [128, 4], F32))
        sqc = e(nc.sbuf_tensor("sqc", [128, NT], F32))
        mcol = e(nc.sbuf_tensor("mcol", [128, NT], F32))
        vcol = e(nc.sbuf_tensor("vcol", [128, NT], F32))
        tcol = sumc
        rstd = vcol
        nmr = tcol

        pb = [e(nc.psum_tensor(f"pb{i}", [128, 512], F32)) for i in range(4)]
        psA = e(nc.psum_tensor("psA", [128, 1024], F32))
        psB = e(nc.psum_tensor("psB", [128, 1024], F32))
        psum_rot = [(pb[0], 0), (pb[1], 0), (pb[2], 0), (pb[3], 0),
                    (psA, 0), (psA, 512), (psB, 0), (psB, 512)]

        P = Prog(nc, ctx)
        s_ldc = P.sem("ldc")
        s_ldx = P.sem("ldx")
        s_ldw = P.sem("ldw")
        s_pe = P.sem("pe")
        s_ve = P.sem("ve")
        s_ac = P.sem("ac")
        s_gp = P.sem("gp")
        s_st = P.sem("st")
        s_vg = P.sem("vg")
        s_xr = P.sem("xr")
        s_dsc = P.sem("dsc")

        biasr = ktA.bitcast(F32)[:, 4096:5120]
        for dst, src in [(masks, masks_d[:, 0:640]), (idbf, idbf_d), (id8, id8_d),
                         (idf16, idf16_d), (onesbf, onesbf_d)]:
            P.emit("sync", lambda en, d=dst, s=src: en.dma_start(out=d[:], in_=s),
                   incs=[(s_ldc, 16)])
        C_CONST = s_ldc.count

        def wload(dst, src):
            return lambda en: en.dma_start(
                out=dst[:].rearrange("p (ic o) -> p ic o", o=H),
                in_=src.rearrange("(ic p) o -> p ic o", p=128))

        P.emit("gpsimd", wload(wbfA, wqT_d), incs=[(s_ldw, 16)])
        CW_Q = s_ldw.count
        P.emit("gpsimd", wload(wbfB, wkT_d), incs=[(s_ldw, 16)])
        CW_K = s_ldw.count
        P.emit("gpsimd", wload(wf8, wfT_d), incs=[(s_ldw, 16)])
        CW_F = s_ldw.count

        x_sb = qtA.bitcast(F32)
        xn_tok = ktA
        xq_tok = ktA.bitcast(FP8)
        xd3 = x_d.rearrange("(c t p) h -> c p t h", p=128, t=4)

        rot_i = 0
        rot_free = [0] * 8
        chunk_done = {}
        P.emit("vector", lambda en: en.memset(cinvH[:], 1.0 / H))
        P.emit("vector", lambda en: en.memset(cnegH[:], -1.0 / H))
        P.emit("vector", lambda en: en.memset(cepsm[:], float(EPS)), incs=[(s_ve, 1)])
        C_MEMS = s_ve.count

        kt_f32 = ktA.bitcast(F32)

        def xslot(c):
            if c == 2:
                return kt_f32[:, 4096:8192]
            sl = 0 if c == 3 else c
            return x_sb[:, sl * 4096:(sl + 1) * 4096]

        xdma_count = {}
        sq_red_count = {}
        for c in range(3):
            P.emit("sync", lambda en, c=c: en.dma_start(
                out=xslot(c).rearrange("p (t h) -> p t h", h=1024), in_=xd3[c]),
                incs=[(s_ldx, 16)])
            xdma_count[c] = s_ldx.count

        xsq_a = ktA.bitcast(F32)[:, 3072:4096]
        xsq_b = vA.bitcast(F32)[:, 0:1024]
        nmr_cnt = {}
        trans_cnt = {-1: 0}

        def emit_stats(c):
            C_X = xdma_count[c]
            for i in range(4):
                t = 4 * c + i
                xt = xslot(c)[:, i * 1024:(i + 1) * 1024]
                xsq = xsq_a if i % 2 == 0 else xsq_b
                P.emit("vector", lambda en, xt=xt, t=t: en.reduce_sum(
                    sumc[:, t:t + 1], xt, axis=mybir.AxisListType.X),
                    waits=[(s_ldx, C_X)])
                wv_sq = [(s_ldx, C_X)]
                if (c, i) > (0, 1):
                    wv_sq.append((s_ve, sq_red_count[(4 * c + i) - 2]))
                P.emit("scalar", lambda en, xt=xt, xq=xsq: en.activation(
                    xq, xt, AF.Square),
                    waits=wv_sq, incs=[(s_ac, 1)])
                C_SQ = s_ac.count
                P.emit("vector", lambda en, t=t, xq=xsq: en.reduce_sum(
                    sqc[:, t:t + 1], xq, axis=mybir.AxisListType.X),
                    waits=[(s_ac, C_SQ)], incs=[(s_ve, 1)])
                sq_red_count[4 * c + i] = s_ve.count
            C_RED = s_ve.count
            sl = slice(4 * c, 4 * c + 4)
            P.emit("gpsimd", lambda en, sl=sl: en.tensor_mul(mcol[:, sl], sumc[:, sl], cnegH[:, 0:4]),
                   waits=[(s_ve, C_RED), (s_ve, C_MEMS)])
            P.emit("gpsimd", lambda en, sl=sl: en.tensor_mul(tcol[:, sl], mcol[:, sl], mcol[:, sl]))
            P.emit("gpsimd", lambda en, sl=sl: en.tensor_mul(vcol[:, sl], sqc[:, sl], cinvH[:, 0:4]))
            P.emit("gpsimd", lambda en, sl=sl: en.tensor_add(vcol[:, sl], vcol[:, sl], cepsm[:, 0:4]))
            P.emit("gpsimd", lambda en, sl=sl: en.tensor_sub(vcol[:, sl], vcol[:, sl], tcol[:, sl]),
                   incs=[(s_gp, 1)])
            C_V = s_gp.count
            P.emit("vector", lambda en, sl=sl: en.reciprocal(tcol[:, sl], vcol[:, sl]),
                   waits=[(s_gp, C_V)], incs=[(s_ve, 1)])
            C_RV = s_ve.count
            P.emit("scalar", lambda en, sl=sl: en.activation(rstd[:, sl], tcol[:, sl], AF.Sqrt),
                   waits=[(s_ve, C_RV)], incs=[(s_ac, 1)])
            C_RS = s_ac.count
            P.emit("gpsimd", lambda en, sl=sl: en.tensor_mul(nmr[:, sl], mcol[:, sl], rstd[:, sl]),
                   waits=[(s_ac, C_RS)], incs=[(s_gp, 1)])
            nmr_cnt[c] = s_gp.count

        def emit_apply(c):
            nonlocal rot_i
            for i in range(4):
                t = 4 * c + i
                xt = xslot(c)[:, i * 1024:(i + 1) * 1024]
                wv_ln = [(s_gp, nmr_cnt[c])]
                if i == 0 and trans_cnt[c - 1] > 0:
                    wv_ln.append((s_pe, trans_cnt[c - 1]))
                P.emit("scalar", lambda en, xt=xt, i=i, t=t: en.activation(
                    xn_tok[:, i * 1024:(i + 1) * 1024], xt, AF.Identity,
                    scale=rstd[:, t:t + 1], bias=nmr[:, t:t + 1]),
                    waits=wv_ln)
                P.emit("scalar", lambda en, xt=xt, i=i, t=t: en.activation(
                    xq_tok[:, 8192 + i * 1024: 8192 + (i + 1) * 1024], xt, AF.Identity,
                    scale=rstd[:, t:t + 1], bias=nmr[:, t:t + 1]),
                    incs=[(s_ac, 1)])
            C_LN = s_ac.count

            for kind in range(2):
                for f in range(NF):
                    bi = rot_i % 8
                    bank, bb0 = psum_rot[bi]
                    for i in range(4):
                        wv = []
                        if i == 0:
                            wv = [(s_ac, C_LN), (s_ldc, C_CONST)]
                            if rot_free[bi] > 0:
                                wv.append((s_ve, rot_free[bi]))
                        if kind == 0:
                            lhsT = xn_tok[:, i * 1024 + f * 128:i * 1024 + (f + 1) * 128]
                            rhs = idbf[:]
                        else:
                            lhsT = xq_tok[:, 8192 + i * 1024 + f * 128: 8192 + i * 1024 + (f + 1) * 128]
                            rhs = id8[:]
                        P.emit("tensor", lambda en, b=bank, b0=bb0, i=i, l=lhsT, r=rhs: en.matmul(
                            b[:, b0 + i * 128:b0 + (i + 1) * 128], l, r, start=True, stop=True),
                            waits=wv, incs=[(s_pe, 1)] if i == 3 else ())
                    C_T = s_pe.count
                    dst = (xnT if kind == 0 else xqT)[:, f * S + c * 512: f * S + (c + 1) * 512]
                    P.emit("vector", lambda en, b=bank, b0=bb0, d=dst: en.tensor_copy(d, b[:, b0:b0 + 512]),
                           waits=[(s_pe, C_T)], incs=[(s_ve, 1)])
                    rot_free[bi] = s_ve.count
                    rot_i += 1
            trans_cnt[c] = s_pe.count
            chunk_done[c] = s_ve.count

        emit_stats(0)
        emit_stats(1)
        emit_apply(0)
        P.emit("sync", lambda en: en.dma_start(
            out=xslot(3).rearrange("p (t h) -> p t h", h=1024), in_=xd3[3]),
            waits=[(s_ve, chunk_done[0])], incs=[(s_ldx, 16)])
        xdma_count[3] = s_ldx.count
        emit_stats(2)
        emit_apply(1)
        emit_stats(3)
        emit_apply(2)
        emit_apply(3)
        C_P0ACT = s_ac.count
        C_P0 = s_ve.count
        C_DBG_P0 = 0
        if debug:
            def dump_stats(en):
                return en.dma_start(out=dbg["d_stats"].rearrange("p (k t) -> p k t", t=NT),
                                    in_=bass.AP(sumc, 0, [[NT, 128], [0, 1], [1, NT]]))
            for ii, tens in enumerate([sumc, sqc, mcol, vcol, rstd, nmr]):
                P.emit("gpsimd", lambda en, ii=ii, tn=tens: en.dma_start(
                    out=dbg["d_stats"][:, ii * NT:(ii + 1) * NT], in_=tn[:]),
                    waits=[(s_ve, C_P0)], incs=[(s_vg, 16)])
            P.emit("gpsimd", lambda en: en.dma_start(out=dbg["d_xnT"], in_=xnT[:]),
                   waits=[(s_ve, C_P0)], incs=[(s_vg, 16)])
            P.emit("gpsimd", lambda en: en.dma_start(out=dbg["d_xqT"], in_=xqT[:]),
                   incs=[(s_vg, 16)])
            C_DBG_P0 = s_vg.count

        if phase_limit == "p0":
            P.emit("sync", lambda en: en.nop(), waits=[(s_vg, s_vg.count)])
            P.finalize()
            return nc

        bank_free = {0: 0, 1: 0}

        def gemm_T(wsb, out_sb, wait_w):
            for o in range(NF):
                banks = [psum_rot[(o % 2) * 4 + j] for j in range(4)]
                for ic in range(NF):
                    lhsT = wsb[:, ic * H + o * 128: ic * H + (o + 1) * 128]
                    for sc in range(4):
                        wv = []
                        if ic == 0 and sc == 0:
                            wv = [(s_ldw, wait_w), (s_ve, C_P0)]
                            if bank_free[o % 2] > 0:
                                wv.append((s_ve, bank_free[o % 2]))
                        bt, bb0 = banks[sc]
                        P.emit("tensor", lambda en, b=bt, b0=bb0, l=lhsT, sc=sc, ic=ic: en.matmul(
                            b[:, b0:b0 + 512], l, xnT[:, ic * S + sc * 512: ic * S + (sc + 1) * 512],
                            start=(ic == 0), stop=(ic == NF - 1)),
                            waits=wv,
                            incs=[(s_pe, 1)] if (ic == NF - 1 and sc == 3) else ())
                C_G = s_pe.count
                for sc in range(4):
                    dst = out_sb[:, o * S + sc * 512: o * S + (sc + 1) * 512]
                    bt, bb0 = banks[sc]
                    P.emit("vector", lambda en, b=bt, b0=bb0, d=dst: en.tensor_copy(d, b[:, b0:b0 + 512]),
                           waits=[(s_pe, C_G)] if sc == 0 else (), incs=[(s_ve, 1)])
                bank_free[o % 2] = s_ve.count
            return s_ve.count

        C_QT = gemm_T(wbfA, QT, CW_Q)
        P.emit("gpsimd", wload(wbfA, wvT_d),
               waits=[(s_pe, s_pe.count)], incs=[(s_ldw, 16)])
        CW_V = s_ldw.count
        C_KT = gemm_T(wbfB, KT, CW_K)
        P.emit("gpsimd", wload(wbfB, woT_d),
               waits=[(s_pe, s_pe.count)], incs=[(s_ldw, 16)])
        CW_O = s_ldw.count

        P.emit("vector", lambda en: en.memset(V[:], 1.0), waits=[(s_ac, C_P0ACT)])
        P.emit("vector", lambda en: en.memset(zb, 0.0), incs=[(s_ve, 1)])
        C_VONE = s_ve.count
        vbank_free = {0: 0, 1: 0}
        C_V0 = 0
        for t in range(NT):
            banks = [psum_rot[(t % 2) * 2 + j][0] for j in range(2)]
            for ic in range(NF):
                lhsT = xnT[:, ic * S + t * 128: ic * S + (t + 1) * 128]
                for oc in range(2):
                    wv = []
                    if ic == 0 and oc == 0:
                        wv = [(s_ldw, CW_V), (s_ve, C_VONE), (s_ve, C_KT)]
                        if vbank_free[t % 2] > 0:
                            wv.append((s_ve, vbank_free[t % 2]))
                    P.emit("tensor", lambda en, b=banks[oc], l=lhsT, ic=ic, oc=oc: en.matmul(
                        b[:], l, wbfA[:, ic * H + oc * 512: ic * H + (oc + 1) * 512],
                        start=(ic == 0), stop=(ic == NF - 1)),
                        waits=wv,
                        incs=[(s_pe, 1)] if (ic == NF - 1 and oc == 1) else ())
            C_VG = s_pe.count
            for oc in range(2):
                dstap = V[:, t * 1040 + oc * 520: t * 1040 + (oc + 1) * 520].rearrange(
                    "p (h d) -> p h d", d=65)[:, :, 0:64]
                srcap = banks[oc][:].rearrange("p (h d) -> p h d", d=64)
                zap = zb.rearrange("p (h d) -> p h d", d=64)
                P.emit("vector", lambda en, sp=srcap, dp=dstap, zp=zap: en.tensor_tensor(dp, sp, zp, ALU.add),
                       waits=[(s_pe, C_VG)] if oc == 0 else (), incs=[(s_ve, 1)])
            vbank_free[t % 2] = s_ve.count
            if t == 0:
                C_V0 = s_ve.count
        C_VALL = s_ve.count
        C_DBG_P1 = 0
        if debug:
            P.emit("gpsimd", lambda en: en.dma_start(out=dbg["d_qT"], in_=QT[:]),
                   waits=[(s_ve, C_VALL)], incs=[(s_vg, 16)])
            P.emit("gpsimd", lambda en: en.dma_start(out=dbg["d_kT"], in_=KT[:]),
                   incs=[(s_vg, 16)])
            P.emit("gpsimd", lambda en: en.dma_start(out=dbg["d_v"], in_=V[:]),
                   incs=[(s_vg, 16)])
            C_DBG_P1 = s_vg.count

        P.emit("vector", lambda en: en.memset(V0c[:], 0.0), waits=[(s_ve, C_P0)],
               incs=[(s_ve, 1)])
        C_V0CM = s_ve.count
        for h in range(NH):
            hc = h // 2
            dst = bass.AP(V0c, ((h % 2) * 64) * 128 + hc * 16 + h, [[128, 64], [1, 1]])
            P.emit("gpsimd", lambda en, d=dst, h=h: en.dma_start(
                out=d, in_=V[0:1, h * 65: h * 65 + 64]),
                waits=[(s_ve, C_V0), (s_ve, C_V0CM)] if h == 0 else (), incs=[(s_vg, 16)])
        C_VGC = s_vg.count

        GC0, GQ0, EQ0 = 0, 776, 904
        P.emit("vector", lambda en: en.memset(U[:, 0:GQ0 + 128], 0.0), waits=[(s_ve, C_KT)])
        for cch in range(8):
            colA = 32 * ((2 * cch) % 4)
            colB = 32 * ((2 * cch + 1) % 4)
            P.emit("vector", lambda en, c=cch, col=colA: en.tensor_copy(
                U[0:64, GC0 + c * 97 + col: GC0 + c * 97 + col + 1], KT[0:64, c * S: c * S + 1]))
            P.emit("vector", lambda en, c=cch, col=colB: en.tensor_copy(
                U[64:128, GC0 + c * 97 + col: GC0 + c * 97 + col + 1], KT[64:128, c * S: c * S + 1]))
            P.emit("vector", lambda en, c=cch: en.tensor_copy(
                U[0:64, GQ0 + c * 16 + 2 * c: GQ0 + c * 16 + 2 * c + 1], QT[0:64, c * S: c * S + 1]),
                waits=[(s_ve, C_QT)] if cch == 0 else ())
            P.emit("vector", lambda en, c=cch: en.tensor_copy(
                U[64:128, GQ0 + c * 16 + 2 * c + 1: GQ0 + c * 16 + 2 * c + 2], QT[64:128, c * S: c * S + 1]),
                incs=[(s_ve, 1)] if cch == 7 else ())
        C_GBUILD = s_ve.count

        for tt in range(NT):
            for cch in range(NF):
                wv = []
                if tt == 0 and cch == 0:
                    wv = [(s_ve, C_GBUILD), (s_ve, C_VALL)]
                P.emit("tensor", lambda en, tt=tt, c=cch: en.matmul(
                    psA[:, tt * 16:(tt + 1) * 16],
                    KT[:, c * S + tt * 128: c * S + (tt + 1) * 128],
                    U[:, GQ0 + c * 16: GQ0 + (c + 1) * 16],
                    start=(c == 0), stop=(c == NF - 1)),
                    waits=wv,
                    incs=[(s_pe, 1)] if (tt == NT - 1 and cch == NF - 1) else ())
        C_S0Q = s_pe.count
        P.emit("scalar", lambda en: en.activation(U[:, EQ0:EQ0 + 256], psA[:, 0:256], AF.Exp, scale=float(SCALE)),
               waits=[(s_pe, C_S0Q)], incs=[(s_ac, 1)])
        C_E0Q = s_ac.count

        e0_wait = {0: 0, 1: 0}
        for wv_i in range(4):
            for sc in range(4):
                k = wv_i * 4 + sc
                pbase = 0 if k % 2 == 0 else 512
                wvl = [(s_ve, C_GBUILD)]
                if e0_wait[k % 2] > 0:
                    wvl.append((s_ac, e0_wait[k % 2]))
                for ci, cch in enumerate((2 * wv_i, 2 * wv_i + 1)):
                    P.emit("tensor", lambda en, c=cch, sc=sc, p0=pbase, ci=ci: en.matmul(
                        psB[0:97, p0:p0 + 512], U[:, GC0 + c * 97: GC0 + (c + 1) * 97],
                        QT[:, c * S + sc * 512: c * S + (sc + 1) * 512],
                        start=(ci == 0), stop=(ci == 1)),
                        waits=wvl if ci == 0 else (), incs=[(s_pe, 1)] if ci == 1 else ())
                C_E0M = s_pe.count
                P.emit("scalar", lambda en, w=wv_i, sc=sc, p0=pbase: en.activation(
                    E0[0:97, w * S + sc * 512: w * S + (sc + 1) * 512], psB[0:97, p0:p0 + 512],
                    AF.Exp, scale=float(SCALE)),
                    waits=[(s_pe, C_E0M)], incs=[(s_ac, 1)])
                e0_wait[k % 2] = s_ac.count
        C_E0 = s_ac.count

        out0_first = True
        for tt in range(2, NT):
            for ci, (pt, pbase, vc0, w) in enumerate(
                    [(psA, 0, 0, 512), (psA, 512, 512, 512), (psB, 0, 1024, 16)]):
                wv = []
                if out0_first:
                    wv = [(s_ac, C_E0Q), (s_ac, C_E0), (s_ve, C_VALL)]
                    out0_first = False
                P.emit("tensor", lambda en, pt=pt, p0=pbase, tt=tt, vc0=vc0, w=w: en.matmul(
                    pt[0:16, p0:p0 + w],
                    U[:, EQ0 + tt * 16: EQ0 + (tt + 1) * 16],
                    V[:, tt * 1040 + vc0: tt * 1040 + vc0 + w],
                    start=(tt == 2), stop=(tt == NT - 1)),
                    waits=wv, incs=[(s_pe, 1)] if (tt == NT - 1 and ci == 2) else ())
        C_OUT0MM = s_pe.count
        C_DBG_E0 = 0
        if debug:
            P.emit("gpsimd", lambda en: en.dma_start(out=dbg["d_e0"], in_=E0[0:97, 0:4 * S]),
                   waits=[(s_ac, C_E0)], incs=[(s_vg, 16)])
            P.emit("gpsimd", lambda en: en.dma_start(out=dbg["d_e0q"], in_=U[:, EQ0:EQ0 + 256]),
                   waits=[(s_ac, C_E0Q)], incs=[(s_vg, 16)])
            C_DBG_E0 = s_vg.count
        wv = [(s_pe, C_OUT0MM)]
        if debug and C_DBG_E0 > 0:
            wv.append((s_vg, C_DBG_E0))
        P.emit("vector", lambda en: en.memset(U[0:64, :], 0.0), waits=wv)
        P.emit("vector", lambda en: en.memset(U[64:128, :], 0.0))
        P.emit("vector", lambda en: en.tensor_copy(U[0:16, 0:1024], psA[0:16, :]))
        P.emit("vector", lambda en: en.tensor_copy(U[0:16, 1024:1040], psB[0:16, 0:16]),
               incs=[(s_ve, 1)])
        C_OUT0SB = s_ve.count

        P.emit("vector", lambda en: en.memset(E0sp[:], 0.0))
        P.emit("vector", lambda en: en.memset(Odsp[:], 0.0), incs=[(s_ve, 1)])
        C_PREMS = s_ve.count
        for h in range(NH):
            P.emit("gpsimd", lambda en, h=h: en.dma_start(
                out=Odsp[0:1, h * 16:h * 16 + 1], in_=U[h:h + 1, h * 65 + 64: h * 65 + 65]),
                waits=[(s_ve, C_OUT0SB), (s_ve, C_PREMS)] if h == 0 else (), incs=[(s_vg, 16)])
        for h in range(NH):
            er = 32 * (h % 4)
            w = h // 4
            src = bass.AP(E0, er * (NF * H) + w * S + 256, [[NF * H, 1], [1, 1792]])
            dst = bass.AP(E0sp, 16 * 256 + h * 16, [[256, 112], [1, 16]])
            P.emit("gpsimd", lambda en, s=src, d=dst: en.dma_start(out=d, in_=s),
                   waits=[(s_ac, C_E0), (s_ve, C_PREMS)] if h == 0 else (), incs=[(s_vg, 16)])
        C_ESP = s_vg.count
        P.emit("tensor", lambda en: en.matmul(
            psB[:, 0:128], rc2[:, 0:128], rc2[:, 0:128], start=True, stop=False),
            waits=[(s_ve, C_OUT0SB)], incs=[(s_pe, 1)])
        for h in range(NH):
            hr = (h % 2) * 64
            col = (h // 2) * 16 + h
            P.emit("tensor", lambda en, h=h, hr=hr, col=col: en.matmul(
                psB[hr:hr + 64, col:col + 1], U[0:16, h * 65: h * 65 + 64],
                idbf[0:16, h:h + 1], start=False, stop=False,
                tile_position=(0, hr)))
        P.emit("tensor", lambda en: en.matmul(
            psB[:, 128:256], rc2[:, 0:128], rc2[:, 0:128], start=False, stop=True),
            incs=[(s_pe, 1)])
        C_W1C = s_pe.count
        P.emit("vector", lambda en: en.tensor_copy(W1c[:], psB[:, 0:128]),
               waits=[(s_pe, C_W1C)], incs=[(s_ve, 1)])
        C_W1CE = s_ve.count
        for sc in range(2):
            for hc in range(NF):
                wv = []
                if sc == 0 and hc == 0:
                    wv = [(s_vg, C_VGC), (s_ldw, CW_O), (s_ve, C_OUT0SB)]
                P.emit("tensor", lambda en, sc=sc, hc=hc: en.matmul(
                    psA[0:16, sc * 512:(sc + 1) * 512], V0c[:, hc * 16:(hc + 1) * 16],
                    wbfB[:, hc * H + sc * 512: hc * H + (sc + 1) * 512],
                    start=(hc == 0), stop=(hc == NF - 1)),
                    waits=wv, incs=[(s_pe, 1)] if (sc == 1 and hc == NF - 1) else ())
        C_W0MM = s_pe.count
        P.emit("vector", lambda en: en.memset(W01[:, 0:1024], 0.0),
               waits=[(s_vg, C_ESP)])
        P.emit("vector", lambda en: en.tensor_copy(W01[0:16, 0:1024], psA[0:16, 0:1024]),
               waits=[(s_pe, C_W0MM)], incs=[(s_ve, 1)])
        C_W0E = s_ve.count
        for sc in range(2):
            for hc in range(NF):
                wv = []
                if sc == 0 and hc == 0:
                    wv = [(s_ve, C_W0E), (s_ve, C_W1CE)]
                P.emit("tensor", lambda en, sc=sc, hc=hc: en.matmul(
                    psA[0:16, sc * 512:(sc + 1) * 512], W1c[:, hc * 16:(hc + 1) * 16],
                    wbfB[:, hc * H + sc * 512: hc * H + (sc + 1) * 512],
                    start=(hc == 0), stop=(hc == NF - 1)),
                    waits=wv, incs=[(s_pe, 1)] if (sc == 1 and hc == NF - 1) else ())
        C_W1MM = s_pe.count
        P.emit("vector", lambda en: en.tensor_copy(W01[32:48, 0:1024], psA[0:16, 0:1024]),
               waits=[(s_pe, C_W1MM)], incs=[(s_ve, 1)])
        C_W01 = s_ve.count

        ps_free = [C_E0Q, C_E0]
        ebuf_free = [0, 0, 0, 0]
        pb_free = [[], [], [], []]
        pending_scq3 = []
        last_evq2 = [0]
        mul_done = {}
        rc2a_done = {}
        rc2b_done = {}
        chain_in2a = {}
        chain_in2b = {}
        last_gsp = [None]
        pending_tail = []
        pending_ch1 = []
        pending_ch2 = []
        recipa_done = {}
        recipb_done = {}
        dspr_free = {}
        rc2_read = {}
        pending_norm = []
        sinfo = {}
        bank_open = set()
        touched = set()
        slots = [psA, psB]
        DEPTH = 3
        NPS = NH * NT // 2

        def emit_mul_job(hp_j, x, q):
            hr2 = x * 64
            dst = attnT[hr2:hr2 + 64, hp_j * S + q * 512: hp_j * S + (q + 1) * 512]
            meng, msem = ("vector", s_ve) if x == 0 else ("gpsimd", s_gp)
            wv2 = [(s_dsc, (rc2b_done if q == 3 else rc2a_done)[(hp_j, x)])]
            if debug and C_DBG_P0 > 0:
                wv2.append((s_vg, C_DBG_P0))
            P.emit(meng, lambda en, d=dst, hr2=hr2, q=q: en.tensor_tensor(
                d, scA[hr2:hr2 + 64, q * 512:(q + 1) * 512],
                rc2[hr2:hr2 + 64, q * 512:(q + 1) * 512], ALU.mult),
                waits=wv2, incs=[(msem, 1)])
            mul_done[(x, q)] = (msem, msem.count)
            rc2_read[x] = (msem, msem.count)

        def emit_chain1(hp_j, x, dn):
            h = 2 * hp_j + x
            p0 = 64 + 32 * x
            xs = 16 * x
            prevkey = (hp_j - 1, x)
            row_ap = bass.AP(U, p0 * S, [[S, 1], [1, S]])
            wv_cp = [(s_vg, C_ESP)]
            if prevkey in recipb_done:
                wv_cp.append((s_ve, recipb_done[prevkey]))
            P.emit("gpsimd", lambda en, h=h, xs=xs: en.tensor_copy(
                Dsp[:, xs:xs + 16], E0sp[:, h * 16:(h + 1) * 16]),
                waits=wv_cp)
            P.emit("gpsimd", lambda en, h=h, xs=xs: en.tensor_add(
                Dsp[0:1, xs:xs + 16], Dsp[0:1, xs:xs + 16],
                Odsp[0:1, h * 16:(h + 1) * 16]),
                incs=[(s_gp, 1)])
            P.emit("gpsimd", lambda en, ap=row_ap, xs=xs: en.dma_start(
                out=Dsp[:, xs:xs + 16], in_=ap, accum_op=ALU.add),
                waits=[(s_ve, last_evq2[0]), (s_ac, dn)], incs=[(s_vg, 16)])
            chain_in2b[(hp_j, x)] = s_vg.count
            chain_in2a[(hp_j, x)] = s_vg.count
            pending_ch2.append((hp_j, x))

        def emit_chain2(hp_j, x):
            xs = 16 * x
            prevkey = (hp_j - 1, x)
            wv_r = [(s_vg, chain_in2b[(hp_j, x)])]
            if prevkey in dspr_free:
                wv_r.extend(dspr_free[prevkey])
            P.emit("vector", lambda en, xs=xs: en.reciprocal(
                Dspr[:, xs:xs + 16], Dsp[:, xs:xs + 16]),
                waits=wv_r, incs=[(s_ve, 1)])
            recipb_done[(hp_j, x)] = s_ve.count
            recipa_done[(hp_j, x)] = s_ve.count
            P.emit("sync", lambda en, x=x, xs=xs: en.dma_start(
                out=dsprd[x:x + 1, :], in_=Dspr[:, xs:xs + 16]),
                waits=[(s_ve, recipb_done[(hp_j, x)])], incs=[(s_dsc, 16)])
            c_td = s_dsc.count
            wv_f = [(s_dsc, c_td)]
            if x in rc2_read:
                wv_f.append(rc2_read[x])
            rc2_ap = bass.AP(rc2, x * 64 * S, [[S, 64], [1, S]])
            src_ap = bass.AP(dsprd.tensor, x * S, [[0, 64], [1, S]])
            P.emit("sync", lambda en, d=rc2_ap, s=src_ap: en.dma_start(out=d, in_=s),
                   waits=wv_f, incs=[(s_dsc, 16)])
            rc2a_done[(hp_j, x)] = s_dsc.count
            rc2b_done[(hp_j, x)] = s_dsc.count
            dspr_free[(hp_j, x)] = [(s_dsc, c_td)]
            pending_tail.append((hp_j, x))

        def emit_g_tail(hp_j, x):
            h = 2 * hp_j + x
            key = (hp_j, x)
            wv_g = [(s_ve, recipb_done[key])]
            if last_gsp[0] is not None:
                wv_g.append(last_gsp[0])
            P.emit("gpsimd", lambda en, h=h, x=x: en.tensor_mul(
                Gsp[:], E0sp[:, h * 16:(h + 1) * 16], Dspr[:, 16 * x:16 * x + 16]),
                waits=wv_g, incs=[(s_gp, 1)])
            c_gm = s_gp.count
            P.emit("gpsimd", lambda en, h=h: en.dma_start(
                out=U[h:h + 1, :], in_=Gsp[:]),
                waits=[(s_gp, c_gm)], incs=[(s_vg, 16)])
            c_gg = s_vg.count
            P.emit("gpsimd", lambda en, h=h, x=x: en.dma_start(
                out=U[32 + h:33 + h, 0:1], in_=Dspr[0:1, 16 * x:16 * x + 1]),
                incs=[(s_vg, 16)])
            last_gsp[0] = (s_vg, c_gg)
            dspr_free[key].extend([(s_gp, c_gm), (s_vg, s_vg.count)])

        def emit_front(pst):
            hp, tc = divmod(pst, NT)
            slot = pst % 2
            eslot = pst % 4
            sl = slots[slot]
            if tc == 0:
                q0, q1, ocol, ncol = 0, 2, 128, 256
            elif tc == NT - 1:
                q0, q1, ocol, ncol = tc - 1, tc + 1, 0, 256
            else:
                q0, q1, ocol, ncol = tc - 1, tc + 2, 0, 384
            wv = [(s_ve, C_W01)]
            if ps_free[slot] > 0:
                wv.append((s_ac, ps_free[slot]))
            for x in range(2):
                hr = x * 64
                lhsT = KT[hr:hr + 64, hp * S + tc * 128: hp * S + (tc + 1) * 128]
                rhs = QT[hr:hr + 64, hp * S + q0 * 128: hp * S + q1 * 128]
                P.emit("tensor", lambda en, sl=sl, l=lhsT, r=rhs, oc=x * 512 + ocol, nn=ncol, hr=hr: en.matmul(
                    sl[:, oc:oc + nn], l, r, start=True, stop=True, tile_position=(hr, 0)),
                    waits=wv if x == 0 else (), incs=[(s_pe, 1)] if x == 1 else ())
            C_SC = s_pe.count
            wv_e = [(s_pe, C_SC)]
            if ebuf_free[eslot] > 0:
                wv_e.append((s_pe, ebuf_free[eslot]))
            P.emit("scalar", lambda en, sl=sl, es=eslot: en.activation(
                Esb[:, es * 896: es * 896 + 896], sl[:, 0:896],
                AF.Exp, scale=float(SCALE)),
                waits=wv_e, incs=[(s_ac, 1)])
            ps_free[slot] = s_ac.count
            C_EXP = s_ac.count
            minfo = []
            for x in range(2):
                base = eslot * 896 + x * 512
                if tc == 0:
                    eap = Esb[:, base + 256: base + 384]
                    map_ = masks[:, 512:640]
                elif tc == 1:
                    eap = Esb[:, base: base + 384].rearrange("p (a b) -> p a b", b=128)[:, 0:3:2, :]
                    map_ = masks[:, 256:512].rearrange("p (a b) -> p a b", b=128)
                elif tc == NT - 1:
                    eap = Esb[:, base: base + 128]
                    map_ = masks[:, 0:128]
                else:
                    eap = Esb[:, base: base + 384].rearrange("p (a b) -> p a b", b=128)[:, 0:3:2, :]
                    map_ = masks[:, 0:256].rearrange("p (a b) -> p a b", b=128)
                eng, sem = (("vector", s_ve) if (pst + x) % 2 == 0 else ("gpsimd", s_gp))
                P.emit(eng, lambda en, eap=eap, map_=map_: en.tensor_tensor(eap, eap, map_, ALU.mult),
                       waits=[(s_ac, C_EXP), (s_ldc, C_CONST)], incs=[(sem, 1)])
                minfo.append((sem, sem.count))
            sinfo[pst] = (hp, tc, eslot, slot, minfo)

        def emit_back(pst):
            hp, tc, eslot, slot, minfo = sinfo.pop(pst)
            if tc == 0:
                blocks = [(128, 0), (256, 1)]
            elif tc == NT - 1:
                blocks = [(0, tc - 1), (128, tc)]
            else:
                blocks = [(0, tc - 1), (128, tc), (256, tc + 1)]
            stop_Tq = (tc - 1) if tc in (4, 8, 12) else (15 if tc == 15 else None)
            for x in range(2):
                h = 2 * hp + x
                lhsT_v = V[:, tc * 1040 + h * 65: tc * 1040 + (h + 1) * 65]
                runs = []
                for ecol, Tq in blocks:
                    q = Tq // 4
                    b = 2 * x + (q % 2)
                    col = (Tq % 4) * 128
                    virgin = (x, Tq) not in touched
                    touched.add((x, Tq))
                    if (runs and runs[-1][1] == b and runs[-1][2] + runs[-1][3] == col
                            and runs[-1][4] == virgin):
                        runs[-1][3] += 128
                        runs[-1][5] = runs[-1][5] or (Tq == stop_Tq)
                    else:
                        runs.append([ecol, b, col, 128, virgin, Tq == stop_Tq])
                first = True
                for ecol, b, col, width, _virgin, stp in runs:
                    fresh = b not in bank_open
                    bank_open.add(b)
                    if stp:
                        bank_open.discard(b)
                        for Tq in range(4 * (stop_Tq // 4), 4 * (stop_Tq // 4) + 4):
                            touched.discard((x, Tq))
                    wv = []
                    if first:
                        wv.append(minfo[x])
                        first = False
                    if fresh:
                        wv.extend(pb_free[b])
                    P.emit("tensor", lambda en, bb=pb[b], l=lhsT_v, es=eslot, x=x, ec=ecol, col=col, wd=width, sf=fresh, sp=stp: en.matmul(
                        bb[0:65, col:col + wd], l,
                        Esb[:, es * 896 + x * 512 + ec: es * 896 + x * 512 + ec + wd],
                        start=sf, stop=sp),
                        waits=wv, incs=[(s_pe, 1)])
            ebuf_free[eslot] = s_pe.count

            if tc in (5, 6, 7, 8):
                for _ in range(2):
                    if pending_norm:
                        hp_j, x_j, q_j = pending_norm.pop(0)
                        emit_mul_job(hp_j, x_j, q_j)

            if tc == 0 and pending_ch1:
                while pending_ch1:
                    emit_chain1(*pending_ch1.pop(0))
            if tc == 1 and pending_ch2:
                while pending_ch2:
                    emit_chain2(*pending_ch2.pop(0))

            if tc == 1 and pending_scq3:
                for b_i, xr, wv_d, dn in pending_scq3:
                    P.emit("vector", lambda en, b=pb[b_i], xr=xr: en.tensor_copy(
                        scA[xr:xr + 64, 3 * 512:4 * 512], b[0:64, :]),
                        waits=wv_d, incs=[(s_ve, 1)])
                    pb_free[b_i] = [(s_ve, s_ve.count), dn]
                pending_scq3.clear()

            evq = {6: 0, 8: 1, 12: 2}.get(tc)
            if evq is not None:
                C_AVB = s_pe.count
                for x in range(2):
                    bk = 2 * x + (evq % 2)
                    p0 = 64 + 32 * x
                    wv_e = [(s_pe, C_AVB)]
                    if (x, evq) in mul_done:
                        wv_e.append(mul_done[(x, evq)])
                    if (hp - 1, x) in chain_in2a:
                        wv_e.append((s_vg, chain_in2a[(hp - 1, x)]))
                    P.emit("vector", lambda en, b=pb[bk], xr=x * 64, q=evq: en.tensor_copy(
                        scA[xr:xr + 64, q * 512:(q + 1) * 512], b[0:64, :]),
                        waits=wv_e)
                    P.emit("vector", lambda en, b=pb[bk], p0=p0, q=evq: en.tensor_copy(
                        U[p0:p0 + 1, q * 512:(q + 1) * 512], b[64:65, :]),
                        incs=[(s_ve, 1)])
                    pb_free[bk] = [(s_ve, s_ve.count)]
                if evq == 2:
                    last_evq2[0] = s_ve.count

            if tc in (5, 6) and pending_tail:
                emit_g_tail(*pending_tail.pop(0))

            if tc == 15:
                C_AVB15 = s_pe.count
                dnq3 = {}
                for x in range(2):
                    bk = 2 * x + 1
                    p0 = 64 + 32 * x
                    wv_dn = [(s_pe, C_AVB15)]
                    if (hp - 1, x) in chain_in2b:
                        wv_dn.append((s_vg, chain_in2b[(hp - 1, x)]))
                    P.emit("scalar", lambda en, b=pb[bk], p0=p0: en.copy(
                        U[p0:p0 + 1, 3 * 512:4 * 512], b[64:65, :]),
                        waits=wv_dn, incs=[(s_ac, 1)])
                    dnq3[x] = s_ac.count
                    wv_sc = [(s_pe, C_AVB15)]
                    if (x, 3) in mul_done:
                        wv_sc.append(mul_done[(x, 3)])
                    pending_scq3.append((bk, x * 64, wv_sc, (s_ac, dnq3[x])))
                for x in range(2):
                    pending_ch1.append((hp, x, dnq3[x]))
                for q in range(4):
                    for x in range(2):
                        pending_norm.append((hp, x, q))

        for i in range(NPS + DEPTH):
            if i < NPS:
                emit_front(i)
            if i >= DEPTH:
                emit_back(i - DEPTH)
        for b_i, xr, wv_d, dn in pending_scq3:
            P.emit("vector", lambda en, b=pb[b_i], xr=xr: en.tensor_copy(
                scA[xr:xr + 64, 3 * 512:4 * 512], b[0:64, :]),
                waits=wv_d, incs=[(s_ve, 1)])
        pending_scq3.clear()
        while pending_ch1:
            emit_chain1(*pending_ch1.pop(0))
        while pending_ch2:
            emit_chain2(*pending_ch2.pop(0))
        C_PSFREE_VE = s_ve.count
        C_PSFREE_AC = max(ps_free) if isinstance(ps_free[0], int) else 0
        while pending_tail:
            emit_g_tail(*pending_tail.pop(0))
        while pending_norm:
            hp_j, x_j, q_j = pending_norm.pop(0)
            emit_mul_job(hp_j, x_j, q_j)
        C_ATTN = s_ve.count
        C_ATTN_GP = s_gp.count
        C_GDONE = s_dsc.count
        C_GDONE_VG = s_vg.count
        C_DBG_P2 = 0
        if debug:
            P.emit("gpsimd", lambda en: en.dma_start(out=dbg["d_attnT"], in_=attnT[:]),
                   waits=[(s_ve, C_ATTN), (s_gp, C_ATTN_GP)], incs=[(s_vg, 16)])
            C_DBG_P2 = s_vg.count

        P.emit("sync", lambda en: en.dma_start(out=biasr, in_=biasr_d),
               waits=[(s_ve, C_ATTN), (s_gp, C_ATTN_GP)], incs=[(s_ldc, 16)])
        C_BIAS = s_ldc.count
        fbank_free = {0: 0, 1: 0}
        for f in range(NF):
            banks = [psum_rot[(f % 2) * 4 + j] for j in range(4)]
            for cc in range(NF):
                lhsT = wf8[:, cc * H + f * 128: cc * H + (f + 1) * 128]
                for sc in range(4):
                    wv = []
                    if cc == 0 and sc == 0:
                        wv = [(s_ldw, CW_F), (s_ve, C_PSFREE_VE), (s_ac, C_PSFREE_AC)]
                        if debug and C_DBG_P2 > 0:
                            wv.append((s_vg, C_DBG_P2))
                        if fbank_free[f % 2] > 0:
                            wv.append((s_ac, fbank_free[f % 2]))
                    bt, bb0 = banks[sc]
                    P.emit("tensor", lambda en, b=bt, b0=bb0, l=lhsT, cc=cc, sc=sc: en.matmul(
                        b[:, b0:b0 + 512], l, xqT[:, cc * S + sc * 512: cc * S + (sc + 1) * 512],
                        start=(cc == 0), stop=(cc == NF - 1)),
                        waits=wv,
                        incs=[(s_pe, 1)] if (cc == NF - 1 and sc == 3) else ())
            C_FG = s_pe.count
            for sc in range(4):
                dst = ffn16[:, f * S + sc * 512: f * S + (sc + 1) * 512]
                bt, bb0 = banks[sc]
                P.emit("scalar", lambda en, b=bt, b0=bb0, d=dst: en.copy(d, b[:, b0:b0 + 512]),
                       waits=[(s_pe, C_FG)] if sc == 0 else (), incs=[(s_ac, 1)])
            fbank_free[f % 2] = s_ac.count
        C_FFN = s_ac.count
        if debug:
            P.emit("gpsimd", lambda en: en.dma_start(out=dbg["d_ffn"], in_=ffn16[:, 0:NF * S]),
                   waits=[(s_ac, C_FFN)], incs=[(s_vg, 16)])

        if debug:
            P.emit("gpsimd", lambda en: en.dma_start(out=dbg["d_ffn"], in_=ffn16[:, 0:NF * S]),
                   waits=[(s_ac, C_FFN)], incs=[(s_vg, 16)])

        wo_bank_free = {}
        xre_used = {}
        tok_free = {}
        out_used = {}
        for sc in range(4):
            ot_slab = otbf[:, (sc % 2) * 4096:(sc % 2) * 4096 + 4096]
            xr_cnt = {}
            for i in range(4):
                st_ = sc * 4 + i
                xsl_i = st_ % 4
                wv = [(s_ldc, C_BIAS), (s_ac, C_FFN)]
                if xre_used.get(xsl_i, 0) > 0:
                    wv.append((s_ve, xre_used[xsl_i]))
                P.emit("vector", lambda en, xs=xsl_i: en.tensor_copy(
                    x_re[:, xs * 1024:(xs + 1) * 1024], biasr),
                    waits=wv, incs=[(s_ve, 1)])
                C_PRE = s_ve.count
                P.emit("gpsimd", lambda en, st_=st_, xs=xsl_i: en.dma_start(
                    out=x_re[:, xs * 1024:(xs + 1) * 1024],
                    in_=x_d[st_ * 128:(st_ + 1) * 128, :], accum_op=ALU.add),
                    waits=[(s_ve, C_PRE)], incs=[(s_xr, 16)])
                xr_cnt[i] = s_xr.count
            for wave in range(2):
                for oo in range(4):
                    o = wave * 4 + oo
                    for hc in range(NF):
                        lhsT = wbfB[:, hc * H + o * 128: hc * H + (o + 1) * 128]
                        wv = []
                        if hc == 0:
                            wv = [(s_ldw, CW_O), (s_ve, C_ATTN), (s_gp, C_ATTN_GP), (s_ac, C_FFN)]
                            if wo_bank_free.get(oo, 0) > 0:
                                wv.append((s_ac, wo_bank_free[oo]))
                        P.emit("tensor", lambda en, b=pb[oo], l=lhsT, hc=hc, sc=sc: en.matmul(
                            b[:], l, attnT[:, hc * S + sc * 512: hc * S + (sc + 1) * 512],
                            start=(hc == 0), stop=False),
                            waits=wv,
                            incs=[(s_pe, 1)] if hc == NF - 1 else ())
                    P.emit("tensor", lambda en, o=o, sc=sc, b=pb[oo]: en.matmul(
                        b[:], W01[:, o * 128:(o + 1) * 128], U[:, sc * 512:(sc + 1) * 512],
                        start=False, stop=True),
                        waits=[(s_vg, C_GDONE_VG), (s_dsc, C_GDONE)], incs=[(s_pe, 1)])
                    C_WO = s_pe.count
                    dsl = ot_slab[:, o * 512:(o + 1) * 512]
                    P.emit("scalar", lambda en, b=pb[oo], d=dsl: en.copy(d, b[:]),
                           waits=[(s_pe, C_WO)], incs=[(s_ac, 1)])
                    wo_bank_free[oo] = s_ac.count
            C_OT = s_ac.count

            for i in range(4):
                st_ = sc * 4 + i
                xsl_i = st_ % 4
                C_XR = xr_cnt[i]

                pair = [(psA, 0), (psA, 512)] if i % 2 == 0 else [(psB, 0), (psB, 512)]
                wv_t = [(s_ac, C_OT)]
                if tok_free.get(i % 2, 0) > 0:
                    wv_t.append((s_ve, tok_free[i % 2]))
                for f in range(NF):
                    pt, pt0 = pair[f // 4]
                    P.emit("tensor", lambda en, pt=pt, p0=pt0, f=f, i=i, ot=ot_slab: en.matmul(
                        pt[:, p0 + (f % 4) * 128:p0 + (f % 4 + 1) * 128],
                        ot[:, f * 512 + i * 128: f * 512 + i * 128 + 128],
                        idbf[:], start=True, stop=False),
                        waits=wv_t if f == 0 else ())
                    P.emit("tensor", lambda en, pt=pt, p0=pt0, f=f, sc=sc, i=i: en.matmul(
                        pt[:, p0 + (f % 4) * 128:p0 + (f % 4 + 1) * 128],
                        ffn16[:, f * S + sc * 512 + i * 128: f * S + sc * 512 + i * 128 + 128],
                        idf16[:], start=False, stop=True),
                        incs=[(s_pe, 1)] if f == NF - 1 else ())
                C_TOK = s_pe.count

                oslot = st_ % 2
                wv_f = [(s_pe, C_TOK), (s_xr, C_XR)]
                if out_used.get(oslot, 0) > 0:
                    wv_f.append((s_st, out_used[oslot]))
                for hlf in range(2):
                    pt, pt0 = pair[hlf]
                    P.emit("vector", lambda en, os_=oslot, xs=xsl_i, hh=hlf, pt=pt, p0=pt0: en.tensor_add(
                        out_f32[:, os_ * 1024 + hh * 512: os_ * 1024 + (hh + 1) * 512],
                        x_re[:, xs * 1024 + hh * 512: xs * 1024 + (hh + 1) * 512],
                        pt[:, p0:p0 + 512]),
                        waits=wv_f if hlf == 0 else (), incs=[(s_ve, 1)])
                xre_used[xsl_i] = s_ve.count
                tok_free[i % 2] = s_ve.count
                C_OUT = s_ve.count
                P.emit("sync", lambda en, st_=st_, os_=oslot: en.dma_start(
                    out=y_d[st_ * 128:(st_ + 1) * 128, :],
                    in_=out_f32[:, os_ * 1024:(os_ + 1) * 1024]),
                    waits=[(s_ve, C_OUT)], incs=[(s_st, 16)])
                out_used[oslot] = s_st.count

        if debug or light:
            P.emit("gpsimd", lambda en: en.dma_start(out=dbg["d_u"], in_=U[:]),
                   waits=[(s_st, s_st.count)], incs=[(s_vg, 16)])
            P.emit("gpsimd", lambda en: en.dma_start(out=dbg["d_rc2"], in_=rc2[:]),
                   incs=[(s_vg, 16)])
            if light:
                P.emit("gpsimd", lambda en: en.dma_start(out=dbg["d_attnT"], in_=attnT[:]),
                       incs=[(s_vg, 16)])
            P.emit("sync", lambda en: en.nop(), waits=[(s_vg, s_vg.count)])
        P.emit("sync", lambda en: en.nop(), waits=[(s_st, s_st.count)])
        P.finalize()
    return nc


def _host_inputs(L0_attn_x, L0_attn_Wq, L0_attn_Wk, L0_attn_Wv, L0_attn_Wo,
                 L0_ffn_weight, L0_ffn_bias):
    x = np.ascontiguousarray(np.asarray(L0_attn_x, dtype=np.float32))
    common = dict(
        wqT=np.ascontiguousarray(np.asarray(L0_attn_Wq, np.float32).T),
        wkT=np.ascontiguousarray(np.asarray(L0_attn_Wk, np.float32).T),
        wvT=np.ascontiguousarray(np.asarray(L0_attn_Wv, np.float32).T),
        woT=np.ascontiguousarray(np.asarray(L0_attn_Wo, np.float32).T),
        wfT=np.ascontiguousarray(np.asarray(L0_ffn_weight, np.float32).T),
        biasr=np.ascontiguousarray(np.broadcast_to(
            np.asarray(L0_ffn_bias, np.float32)[None, :], (128, H))),
        masks=_tri_masks(),
        idbf=np.eye(128, dtype=ml_dtypes.bfloat16),
        id8=np.eye(128, dtype=ml_dtypes.float8_e4m3),
        idf16=np.eye(128, dtype=np.float16),
        onesbf=np.ones((128, 128), dtype=ml_dtypes.bfloat16),
    )
    return x, common


def kernel(L0_attn_x, L0_attn_Wq, L0_attn_Wk, L0_attn_Wv, L0_attn_Wo,
           L0_ffn_weight, L0_ffn_bias):
    x, common = _host_inputs(L0_attn_x, L0_attn_Wq, L0_attn_Wk, L0_attn_Wv,
                             L0_attn_Wo, L0_ffn_weight, L0_ffn_bias)
    nc = build()
    in_maps = [dict(x=np.ascontiguousarray(x[b]), **common) for b in range(8)]
    res = run_bass_kernel_spmd(nc, in_maps, core_ids=list(range(8)))
    return np.stack([r["y"] for r in res.results], axis=0)


# version 24
# speedup vs baseline: 1.0377x; 1.0377x over previous
import numpy as np
import ml_dtypes
import concourse.bass as bass
import concourse.mybir as mybir
from concourse.bass_utils import run_bass_kernel_spmd
from contextlib import ExitStack

F32 = mybir.dt.float32
BF16 = mybir.dt.bfloat16
FP16 = mybir.dt.float16
FP8 = mybir.dt.float8e4
AF = mybir.ActivationFunctionType
ALU = mybir.AluOpType

S = 2048
H = 1024
NH = 16
NT = 16
NF = 8
SCALE = 64.0 ** -0.5
EPS = 1e-5

ENGS = ("tensor", "vector", "scalar", "sync", "gpsimd")


class Sem:
    def __init__(self, handle):
        self.h = handle
        self.count = 0


class Prog:

    def __init__(self, nc, ctx):
        self.nc = nc
        self.ctx = ctx
        self.streams = {e: [] for e in ENGS}
        self.tags = {e: [] for e in ENGS}
        self.sems = {}
        self._last_wait = {}

    def sem(self, name):
        if name not in self.sems:
            self.sems[name] = Sem(self.ctx.enter_context(self.nc.semaphore(name)))
        return self.sems[name]

    def emit(self, eng, fn, waits=(), incs=()):
        import sys as _sys
        wl = []
        for s, v in waits:
            key = (eng, s.h.name)
            if v > 0 and self._last_wait.get(key, -1) < v:
                self._last_wait[key] = v
                wl.append((s, v))
        for s, k in incs:
            s.count += k
        self.streams[eng].append((fn, wl, list(incs)))
        self.tags[eng].append(_sys._getframe(1).f_lineno)

    def finalize(self):
        block = self.ctx.enter_context(self.nc.Block())
        streams = self.streams

        def runner(name):
            def _run(eng):
                with self.nc.allow_low_precision(
                        reason="intentional bf16 stores; accumulation stays fp32"):
                    for fn, waits, incs in streams[name]:
                        for s, v in waits:
                            eng.wait_ge(s.h, v)
                        inst = fn(eng)
                        for s, k in incs:
                            inst.then_inc(s.h, k)
            return _run

        block.tensor(runner("tensor"))
        block.vector(runner("vector"))
        block.scalar(runner("scalar"))
        block.sync(runner("sync"))
        block.gpsimd(runner("gpsimd"))
        self.nc.op_tags = self.tags


def _tri_masks():
    tl = np.arange(128)[:, None]
    sl = np.arange(128)[None, :]
    tri_hi = (tl <= sl).astype(np.float32)
    tri_lo = (tl >= sl).astype(np.float32)
    tri_lo_r0 = np.maximum(tri_lo, np.broadcast_to((tl == 0), (128, 128)).astype(np.float32))
    tri_hi_c0 = np.maximum(tri_hi, np.broadcast_to((sl == 0), (128, 128)).astype(np.float32))
    m = np.zeros((128, 1024), np.float32)
    m[:, 0:128] = tri_hi
    m[:, 128:256] = tri_lo
    m[:, 256:384] = tri_hi_c0
    m[:, 384:512] = tri_lo
    m[:, 512:640] = tri_lo_r0
    m[:, 640:768] = tri_hi
    return m.astype(ml_dtypes.bfloat16)


def build(debug=False, phase_limit=None, light=False):
    nc = bass.Bass("TRN2", target_bir_lowering=False)

    x_d = nc.dram_tensor("x", [S, H], F32, kind="ExternalInput").ap()
    wqT_d = nc.dram_tensor("wqT", [H, H], F32, kind="ExternalInput").ap()
    wkT_d = nc.dram_tensor("wkT", [H, H], F32, kind="ExternalInput").ap()
    wvT_d = nc.dram_tensor("wvT", [H, H], F32, kind="ExternalInput").ap()
    woT_d = nc.dram_tensor("woT", [H, H], F32, kind="ExternalInput").ap()
    wfT_d = nc.dram_tensor("wfT", [H, H], F32, kind="ExternalInput").ap()
    biasr_d = nc.dram_tensor("biasr", [128, H], F32, kind="ExternalInput").ap()
    masks_d = nc.dram_tensor("masks", [128, 1024], BF16, kind="ExternalInput").ap()
    idbf_d = nc.dram_tensor("idbf", [128, 128], BF16, kind="ExternalInput").ap()
    id8_d = nc.dram_tensor("id8", [128, 128], FP8, kind="ExternalInput").ap()
    idf16_d = nc.dram_tensor("idf16", [128, 128], FP16, kind="ExternalInput").ap()
    onesbf_d = nc.dram_tensor("onesbf", [128, 128], BF16, kind="ExternalInput").ap()
    y_d = nc.dram_tensor("y", [S, H], F32, kind="ExternalOutput").ap()
    dsprd = nc.dram_tensor("dsprd", [2, 2048], BF16, kind="ExternalOutput").ap()

    dbg = {}
    if light:
        for name, shape, dt in [("d_u", [128, S], BF16), ("d_rc2", [128, S], BF16),
                                ("d_attnT", [128, NF * S], BF16)]:
            dbg[name] = nc.dram_tensor(name, shape, dt, kind="ExternalOutput").ap()
    if debug:
        for name, shape, dt in [
            ("d_xnT", [128, NF * S], BF16), ("d_xqT", [128, NF * S], FP8),
            ("d_qT", [128, NF * S], BF16), ("d_kT", [128, NF * S], BF16),
            ("d_u", [128, S], BF16), ("d_rc2", [128, S], BF16),
            ("d_v", [128, NT * 1040], BF16), ("d_attnT", [128, NF * S], BF16),
            ("d_ffn", [128, NF * S], FP16), ("d_e0", [97, 4 * S], BF16),
            ("d_e0q", [128, 256], BF16), ("d_stats", [128, 6 * NT], F32),
        ]:
            dbg[name] = nc.dram_tensor(name, shape, dt, kind="ExternalOutput").ap()

    with ExitStack() as ctx:
        e = ctx.enter_context
        xnT = e(nc.sbuf_tensor("xnT_attnT", [128, NF * S], BF16))
        attnT = xnT
        qtA = e(nc.sbuf_tensor("qt_ffn", [128, NF * S], BF16))
        QT = qtA
        ffn16 = qtA.bitcast(FP16)
        ktA = e(nc.sbuf_tensor("kt_xre", [128, NF * S], BF16))
        KT = ktA
        x_re = ktA.bitcast(F32)
        vA = e(nc.sbuf_tensor("v_out", [128, NT * 1040], BF16))
        V = vA
        out_f32 = vA.bitcast(F32)
        wbfA = e(nc.sbuf_tensor("wbf_a", [128, NF * H], BF16))
        E0 = wbfA
        wbfB = e(nc.sbuf_tensor("wbf_b", [128, NF * H], BF16))
        wf8 = e(nc.sbuf_tensor("wf8", [128, NF * H], FP8))
        xqT = e(nc.sbuf_tensor("xqT", [128, NF * S], FP8))
        otbf = xqT.bitcast(BF16)

        masks = e(nc.sbuf_tensor("sb_masks", [128, 640], BF16))
        idbf = e(nc.sbuf_tensor("sb_idbf", [128, 128], BF16))
        id8 = e(nc.sbuf_tensor("sb_id8", [128, 128], FP8))
        idf16 = e(nc.sbuf_tensor("sb_idf16", [128, 128], FP16))
        onesbf = e(nc.sbuf_tensor("sb_onesbf", [128, 128], BF16))
        U = e(nc.sbuf_tensor("U", [128, S], BF16))
        Esb = e(nc.sbuf_tensor("Esb", [128, 4 * 896], BF16))
        scA = e(nc.sbuf_tensor("scA", [128, S], BF16))
        rc2 = e(nc.sbuf_tensor("rc2", [128, S], BF16))
        W01 = wbfA
        zb = rc2[:, 0:512]
        E0sp = e(nc.sbuf_tensor("E0sp", [128, 16 * 16], BF16))
        V0c = e(nc.sbuf_tensor("V0c", [128, 128], BF16))
        W1c = e(nc.sbuf_tensor("W1c", [128, 128], BF16))
        Gsp = e(nc.sbuf_tensor("Gsp", [128, 16], BF16))
        Odsp = e(nc.sbuf_tensor("Odsp", [1, 256], BF16))
        Dspr = e(nc.sbuf_tensor("Dspr", [128, 32], BF16))
        Dsp = id8.bitcast(F32)
        sumc = e(nc.sbuf_tensor("sumc", [128, NT], F32))
        cinvH = e(nc.sbuf_tensor("cinvH", [128, 4], F32))
        cnegH = e(nc.sbuf_tensor("cnegH", [128, 4], F32))
        cepsm = e(nc.sbuf_tensor("cepsm", [128, 4], F32))
        sqc = e(nc.sbuf_tensor("sqc", [128, NT], F32))
        mcol = e(nc.sbuf_tensor("mcol", [128, NT], F32))
        vcol = e(nc.sbuf_tensor("vcol", [128, NT], F32))
        tcol = sumc
        rstd = vcol
        nmr = tcol

        pb = [e(nc.psum_tensor(f"pb{i}", [128, 512], F32)) for i in range(4)]
        psA = e(nc.psum_tensor("psA", [128, 1024], F32))
        psB = e(nc.psum_tensor("psB", [128, 1024], F32))
        psum_rot = [(pb[0], 0), (pb[1], 0), (pb[2], 0), (pb[3], 0),
                    (psA, 0), (psA, 512), (psB, 0), (psB, 512)]

        P = Prog(nc, ctx)
        s_ldc = P.sem("ldc")
        s_ldx = P.sem("ldx")
        s_ldw = P.sem("ldw")
        s_pe = P.sem("pe")
        s_ve = P.sem("ve")
        s_ac = P.sem("ac")
        s_gp = P.sem("gp")
        s_st = P.sem("st")
        s_vg = P.sem("vg")
        s_xr = P.sem("xr")
        s_dsc = P.sem("dsc")

        biasr = ktA.bitcast(F32)[:, 4096:5120]
        for dst, src in [(masks, masks_d[:, 0:640]), (idbf, idbf_d), (id8, id8_d),
                         (idf16, idf16_d), (onesbf, onesbf_d)]:
            P.emit("sync", lambda en, d=dst, s=src: en.dma_start(out=d[:], in_=s),
                   incs=[(s_ldc, 16)])
        C_CONST = s_ldc.count

        def wload(dst, src):
            return lambda en: en.dma_start(
                out=dst[:].rearrange("p (ic o) -> p ic o", o=H),
                in_=src.rearrange("(ic p) o -> p ic o", p=128))

        P.emit("gpsimd", wload(wbfA, wqT_d), incs=[(s_ldw, 16)])
        CW_Q = s_ldw.count
        P.emit("gpsimd", wload(wbfB, wkT_d), incs=[(s_ldw, 16)])
        CW_K = s_ldw.count
        P.emit("gpsimd", wload(wf8, wfT_d), incs=[(s_ldw, 16)])
        CW_F = s_ldw.count

        x_sb = qtA.bitcast(F32)
        xn_tok = ktA
        xq_tok = ktA.bitcast(FP8)
        xd3 = x_d.rearrange("(c t p) h -> c p t h", p=128, t=4)

        rot_i = 0
        rot_free = [0] * 8
        chunk_done = {}
        P.emit("vector", lambda en: en.memset(cinvH[:], 1.0 / H))
        P.emit("vector", lambda en: en.memset(cnegH[:], -1.0 / H))
        P.emit("vector", lambda en: en.memset(cepsm[:], float(EPS)), incs=[(s_ve, 1)])
        C_MEMS = s_ve.count

        kt_f32 = ktA.bitcast(F32)

        def xslot(c):
            if c == 2:
                return kt_f32[:, 4096:8192]
            sl = 0 if c == 3 else c
            return x_sb[:, sl * 4096:(sl + 1) * 4096]

        xdma_count = {}
        sq_red_count = {}
        for c in range(3):
            P.emit("sync", lambda en, c=c: en.dma_start(
                out=xslot(c).rearrange("p (t h) -> p t h", h=1024), in_=xd3[c]),
                incs=[(s_ldx, 16)])
            xdma_count[c] = s_ldx.count

        xsq_a = ktA.bitcast(F32)[:, 3072:4096]
        xsq_b = vA.bitcast(F32)[:, 0:1024]
        nmr_cnt = {}
        trans_cnt = {-1: 0}

        def emit_stats(c):
            C_X = xdma_count[c]
            for i in range(4):
                t = 4 * c + i
                xt = xslot(c)[:, i * 1024:(i + 1) * 1024]
                xsq = xsq_a if i % 2 == 0 else xsq_b
                P.emit("vector", lambda en, xt=xt, t=t: en.reduce_sum(
                    sumc[:, t:t + 1], xt, axis=mybir.AxisListType.X),
                    waits=[(s_ldx, C_X)])
                wv_sq = [(s_ldx, C_X)]
                if (c, i) > (0, 1):
                    wv_sq.append((s_ve, sq_red_count[(4 * c + i) - 2]))
                P.emit("scalar", lambda en, xt=xt, xq=xsq: en.activation(
                    xq, xt, AF.Square),
                    waits=wv_sq, incs=[(s_ac, 1)])
                C_SQ = s_ac.count
                P.emit("vector", lambda en, t=t, xq=xsq: en.reduce_sum(
                    sqc[:, t:t + 1], xq, axis=mybir.AxisListType.X),
                    waits=[(s_ac, C_SQ)], incs=[(s_ve, 1)])
                sq_red_count[4 * c + i] = s_ve.count
            C_RED = s_ve.count
            sl = slice(4 * c, 4 * c + 4)
            P.emit("gpsimd", lambda en, sl=sl: en.tensor_mul(mcol[:, sl], sumc[:, sl], cnegH[:, 0:4]),
                   waits=[(s_ve, C_RED), (s_ve, C_MEMS)])
            P.emit("gpsimd", lambda en, sl=sl: en.tensor_mul(tcol[:, sl], mcol[:, sl], mcol[:, sl]))
            P.emit("gpsimd", lambda en, sl=sl: en.tensor_mul(vcol[:, sl], sqc[:, sl], cinvH[:, 0:4]))
            P.emit("gpsimd", lambda en, sl=sl: en.tensor_add(vcol[:, sl], vcol[:, sl], cepsm[:, 0:4]))
            P.emit("gpsimd", lambda en, sl=sl: en.tensor_sub(vcol[:, sl], vcol[:, sl], tcol[:, sl]),
                   incs=[(s_gp, 1)])
            C_V = s_gp.count
            P.emit("vector", lambda en, sl=sl: en.reciprocal(tcol[:, sl], vcol[:, sl]),
                   waits=[(s_gp, C_V)], incs=[(s_ve, 1)])
            C_RV = s_ve.count
            P.emit("scalar", lambda en, sl=sl: en.activation(rstd[:, sl], tcol[:, sl], AF.Sqrt),
                   waits=[(s_ve, C_RV)], incs=[(s_ac, 1)])
            C_RS = s_ac.count
            P.emit("gpsimd", lambda en, sl=sl: en.tensor_mul(nmr[:, sl], mcol[:, sl], rstd[:, sl]),
                   waits=[(s_ac, C_RS)], incs=[(s_gp, 1)])
            nmr_cnt[c] = s_gp.count

        def emit_apply(c):
            nonlocal rot_i
            for i in range(4):
                t = 4 * c + i
                xt = xslot(c)[:, i * 1024:(i + 1) * 1024]
                wv_ln = [(s_gp, nmr_cnt[c])]
                if i == 0 and trans_cnt[c - 1] > 0:
                    wv_ln.append((s_pe, trans_cnt[c - 1]))
                P.emit("scalar", lambda en, xt=xt, i=i, t=t: en.activation(
                    xn_tok[:, i * 1024:(i + 1) * 1024], xt, AF.Identity,
                    scale=rstd[:, t:t + 1], bias=nmr[:, t:t + 1]),
                    waits=wv_ln)
                P.emit("scalar", lambda en, xt=xt, i=i, t=t: en.activation(
                    xq_tok[:, 8192 + i * 1024: 8192 + (i + 1) * 1024], xt, AF.Identity,
                    scale=rstd[:, t:t + 1], bias=nmr[:, t:t + 1]),
                    incs=[(s_ac, 1)])
            C_LN = s_ac.count

            for kind in range(2):
                for f in range(NF):
                    bi = rot_i % 8
                    bank, bb0 = psum_rot[bi]
                    for i in range(4):
                        wv = []
                        if i == 0:
                            wv = [(s_ac, C_LN), (s_ldc, C_CONST)]
                            if rot_free[bi] > 0:
                                wv.append((s_ve, rot_free[bi]))
                        if kind == 0:
                            lhsT = xn_tok[:, i * 1024 + f * 128:i * 1024 + (f + 1) * 128]
                            rhs = idbf[:]
                        else:
                            lhsT = xq_tok[:, 8192 + i * 1024 + f * 128: 8192 + i * 1024 + (f + 1) * 128]
                            rhs = id8[:]
                        P.emit("tensor", lambda en, b=bank, b0=bb0, i=i, l=lhsT, r=rhs: en.matmul(
                            b[:, b0 + i * 128:b0 + (i + 1) * 128], l, r, start=True, stop=True),
                            waits=wv, incs=[(s_pe, 1)] if i == 3 else ())
                    C_T = s_pe.count
                    dst = (xnT if kind == 0 else xqT)[:, f * S + c * 512: f * S + (c + 1) * 512]
                    P.emit("vector", lambda en, b=bank, b0=bb0, d=dst: en.tensor_copy(d, b[:, b0:b0 + 512]),
                           waits=[(s_pe, C_T)], incs=[(s_ve, 1)])
                    rot_free[bi] = s_ve.count
                    rot_i += 1
            trans_cnt[c] = s_pe.count
            chunk_done[c] = s_ve.count

        emit_stats(0)
        emit_stats(1)
        emit_apply(0)
        P.emit("sync", lambda en: en.dma_start(
            out=xslot(3).rearrange("p (t h) -> p t h", h=1024), in_=xd3[3]),
            waits=[(s_ve, chunk_done[0])], incs=[(s_ldx, 16)])
        xdma_count[3] = s_ldx.count
        emit_stats(2)
        emit_apply(1)
        emit_stats(3)
        emit_apply(2)
        emit_apply(3)
        C_P0ACT = s_ac.count
        C_P0 = s_ve.count
        C_DBG_P0 = 0
        if debug:
            def dump_stats(en):
                return en.dma_start(out=dbg["d_stats"].rearrange("p (k t) -> p k t", t=NT),
                                    in_=bass.AP(sumc, 0, [[NT, 128], [0, 1], [1, NT]]))
            for ii, tens in enumerate([sumc, sqc, mcol, vcol, rstd, nmr]):
                P.emit("gpsimd", lambda en, ii=ii, tn=tens: en.dma_start(
                    out=dbg["d_stats"][:, ii * NT:(ii + 1) * NT], in_=tn[:]),
                    waits=[(s_ve, C_P0)], incs=[(s_vg, 16)])
            P.emit("gpsimd", lambda en: en.dma_start(out=dbg["d_xnT"], in_=xnT[:]),
                   waits=[(s_ve, C_P0)], incs=[(s_vg, 16)])
            P.emit("gpsimd", lambda en: en.dma_start(out=dbg["d_xqT"], in_=xqT[:]),
                   incs=[(s_vg, 16)])
            C_DBG_P0 = s_vg.count

        if phase_limit == "p0":
            P.emit("sync", lambda en: en.nop(), waits=[(s_vg, s_vg.count)])
            P.finalize()
            return nc

        bank_free = {0: 0, 1: 0}

        def gemm_T(wsb, out_sb, wait_w):
            for o in range(NF):
                banks = [psum_rot[(o % 2) * 4 + j] for j in range(4)]
                for ic in range(NF):
                    lhsT = wsb[:, ic * H + o * 128: ic * H + (o + 1) * 128]
                    for sc in range(4):
                        wv = []
                        if ic == 0 and sc == 0:
                            wv = [(s_ldw, wait_w), (s_ve, C_P0)]
                            if bank_free[o % 2] > 0:
                                wv.append((s_ve, bank_free[o % 2]))
                        bt, bb0 = banks[sc]
                        P.emit("tensor", lambda en, b=bt, b0=bb0, l=lhsT, sc=sc, ic=ic: en.matmul(
                            b[:, b0:b0 + 512], l, xnT[:, ic * S + sc * 512: ic * S + (sc + 1) * 512],
                            start=(ic == 0), stop=(ic == NF - 1)),
                            waits=wv,
                            incs=[(s_pe, 1)] if (ic == NF - 1 and sc == 3) else ())
                C_G = s_pe.count
                for sc in range(4):
                    dst = out_sb[:, o * S + sc * 512: o * S + (sc + 1) * 512]
                    bt, bb0 = banks[sc]
                    P.emit("vector", lambda en, b=bt, b0=bb0, d=dst: en.tensor_copy(d, b[:, b0:b0 + 512]),
                           waits=[(s_pe, C_G)] if sc == 0 else (), incs=[(s_ve, 1)])
                bank_free[o % 2] = s_ve.count
            return s_ve.count

        C_QT = gemm_T(wbfA, QT, CW_Q)
        P.emit("gpsimd", wload(wbfA, wvT_d),
               waits=[(s_pe, s_pe.count)], incs=[(s_ldw, 16)])
        CW_V = s_ldw.count
        C_KT = gemm_T(wbfB, KT, CW_K)
        P.emit("gpsimd", wload(wbfB, woT_d),
               waits=[(s_pe, s_pe.count)], incs=[(s_ldw, 16)])
        CW_O = s_ldw.count

        ones_ap = V[:].rearrange("p (t h d) -> p t h d", h=NH, d=65)[:, :, :, 64:65]
        P.emit("vector", lambda en, ap=ones_ap: en.memset(ap, 1.0), waits=[(s_ac, C_P0ACT)])
        P.emit("vector", lambda en: en.memset(zb, 0.0), incs=[(s_ve, 1)])
        C_VONE = s_ve.count
        vbank_free = {0: 0, 1: 0}
        C_V0 = 0
        for t in range(NT):
            banks = [psum_rot[(t % 2) * 2 + j][0] for j in range(2)]
            for ic in range(NF):
                lhsT = xnT[:, ic * S + t * 128: ic * S + (t + 1) * 128]
                for oc in range(2):
                    wv = []
                    if ic == 0 and oc == 0:
                        wv = [(s_ldw, CW_V), (s_ve, C_VONE), (s_ve, C_KT)]
                        if vbank_free[t % 2] > 0:
                            wv.append((s_ve, vbank_free[t % 2]))
                    P.emit("tensor", lambda en, b=banks[oc], l=lhsT, ic=ic, oc=oc: en.matmul(
                        b[:], l, wbfA[:, ic * H + oc * 512: ic * H + (oc + 1) * 512],
                        start=(ic == 0), stop=(ic == NF - 1)),
                        waits=wv,
                        incs=[(s_pe, 1)] if (ic == NF - 1 and oc == 1) else ())
            C_VG = s_pe.count
            for oc in range(2):
                dstap = V[:, t * 1040 + oc * 520: t * 1040 + (oc + 1) * 520].rearrange(
                    "p (h d) -> p h d", d=65)[:, :, 0:64]
                srcap = banks[oc][:].rearrange("p (h d) -> p h d", d=64)
                zap = zb.rearrange("p (h d) -> p h d", d=64)
                P.emit("vector", lambda en, sp=srcap, dp=dstap, zp=zap: en.tensor_tensor(dp, sp, zp, ALU.add),
                       waits=[(s_pe, C_VG)] if oc == 0 else (), incs=[(s_ve, 1)])
            vbank_free[t % 2] = s_ve.count
            if t == 0:
                C_V0 = s_ve.count
        C_VALL = s_ve.count
        C_DBG_P1 = 0
        if debug:
            P.emit("gpsimd", lambda en: en.dma_start(out=dbg["d_qT"], in_=QT[:]),
                   waits=[(s_ve, C_VALL)], incs=[(s_vg, 16)])
            P.emit("gpsimd", lambda en: en.dma_start(out=dbg["d_kT"], in_=KT[:]),
                   incs=[(s_vg, 16)])
            P.emit("gpsimd", lambda en: en.dma_start(out=dbg["d_v"], in_=V[:]),
                   incs=[(s_vg, 16)])
            C_DBG_P1 = s_vg.count

        P.emit("vector", lambda en: en.memset(V0c[:], 0.0), waits=[(s_ve, C_P0)],
               incs=[(s_ve, 1)])
        C_V0CM = s_ve.count
        for h in range(NH):
            hc = h // 2
            dst = bass.AP(V0c, ((h % 2) * 64) * 128 + hc * 16 + h, [[128, 64], [1, 1]])
            P.emit("gpsimd", lambda en, d=dst, h=h: en.dma_start(
                out=d, in_=V[0:1, h * 65: h * 65 + 64]),
                waits=[(s_ve, C_V0), (s_ve, C_V0CM)] if h == 0 else (), incs=[(s_vg, 16)])
        C_VGC = s_vg.count

        GC0, GQ0, EQ0 = 0, 776, 904
        P.emit("vector", lambda en: en.memset(U[:, 0:GQ0 + 128], 0.0), waits=[(s_ve, C_KT)])
        for cch in range(8):
            colA = 32 * ((2 * cch) % 4)
            colB = 32 * ((2 * cch + 1) % 4)
            P.emit("vector", lambda en, c=cch, col=colA: en.tensor_copy(
                U[0:64, GC0 + c * 97 + col: GC0 + c * 97 + col + 1], KT[0:64, c * S: c * S + 1]))
            P.emit("vector", lambda en, c=cch, col=colB: en.tensor_copy(
                U[64:128, GC0 + c * 97 + col: GC0 + c * 97 + col + 1], KT[64:128, c * S: c * S + 1]))
            P.emit("vector", lambda en, c=cch: en.tensor_copy(
                U[0:64, GQ0 + c * 16 + 2 * c: GQ0 + c * 16 + 2 * c + 1], QT[0:64, c * S: c * S + 1]),
                waits=[(s_ve, C_QT)] if cch == 0 else ())
            P.emit("vector", lambda en, c=cch: en.tensor_copy(
                U[64:128, GQ0 + c * 16 + 2 * c + 1: GQ0 + c * 16 + 2 * c + 2], QT[64:128, c * S: c * S + 1]),
                incs=[(s_ve, 1)] if cch == 7 else ())
        C_GBUILD = s_ve.count

        for tt in range(NT):
            for cch in range(NF):
                wv = []
                if tt == 0 and cch == 0:
                    wv = [(s_ve, C_GBUILD), (s_ve, C_VALL)]
                P.emit("tensor", lambda en, tt=tt, c=cch: en.matmul(
                    psA[:, tt * 16:(tt + 1) * 16],
                    KT[:, c * S + tt * 128: c * S + (tt + 1) * 128],
                    U[:, GQ0 + c * 16: GQ0 + (c + 1) * 16],
                    start=(c == 0), stop=(c == NF - 1)),
                    waits=wv,
                    incs=[(s_pe, 1)] if (tt == NT - 1 and cch == NF - 1) else ())
        C_S0Q = s_pe.count
        P.emit("scalar", lambda en: en.activation(U[:, EQ0:EQ0 + 256], psA[:, 0:256], AF.Exp, scale=float(SCALE)),
               waits=[(s_pe, C_S0Q)], incs=[(s_ac, 1)])
        C_E0Q = s_ac.count

        e0_wait = {0: 0, 1: 0}
        for wv_i in range(4):
            for sc in range(4):
                k = wv_i * 4 + sc
                pbase = 0 if k % 2 == 0 else 512
                wvl = [(s_ve, C_GBUILD)]
                if e0_wait[k % 2] > 0:
                    wvl.append((s_ac, e0_wait[k % 2]))
                for ci, cch in enumerate((2 * wv_i, 2 * wv_i + 1)):
                    P.emit("tensor", lambda en, c=cch, sc=sc, p0=pbase, ci=ci: en.matmul(
                        psB[0:97, p0:p0 + 512], U[:, GC0 + c * 97: GC0 + (c + 1) * 97],
                        QT[:, c * S + sc * 512: c * S + (sc + 1) * 512],
                        start=(ci == 0), stop=(ci == 1)),
                        waits=wvl if ci == 0 else (), incs=[(s_pe, 1)] if ci == 1 else ())
                C_E0M = s_pe.count
                P.emit("scalar", lambda en, w=wv_i, sc=sc, p0=pbase: en.activation(
                    E0[0:97, w * S + sc * 512: w * S + (sc + 1) * 512], psB[0:97, p0:p0 + 512],
                    AF.Exp, scale=float(SCALE)),
                    waits=[(s_pe, C_E0M)], incs=[(s_ac, 1)])
                e0_wait[k % 2] = s_ac.count
        C_E0 = s_ac.count

        out0_first = True
        for tt in range(2, NT):
            for ci, (pt, pbase, vc0, w) in enumerate(
                    [(psA, 0, 0, 512), (psA, 512, 512, 512), (psB, 0, 1024, 16)]):
                wv = []
                if out0_first:
                    wv = [(s_ac, C_E0Q), (s_ac, C_E0), (s_ve, C_VALL)]
                    out0_first = False
                P.emit("tensor", lambda en, pt=pt, p0=pbase, tt=tt, vc0=vc0, w=w: en.matmul(
                    pt[0:16, p0:p0 + w],
                    U[:, EQ0 + tt * 16: EQ0 + (tt + 1) * 16],
                    V[:, tt * 1040 + vc0: tt * 1040 + vc0 + w],
                    start=(tt == 2), stop=(tt == NT - 1)),
                    waits=wv, incs=[(s_pe, 1)] if (tt == NT - 1 and ci == 2) else ())
        C_OUT0MM = s_pe.count
        C_DBG_E0 = 0
        if debug:
            P.emit("gpsimd", lambda en: en.dma_start(out=dbg["d_e0"], in_=E0[0:97, 0:4 * S]),
                   waits=[(s_ac, C_E0)], incs=[(s_vg, 16)])
            P.emit("gpsimd", lambda en: en.dma_start(out=dbg["d_e0q"], in_=U[:, EQ0:EQ0 + 256]),
                   waits=[(s_ac, C_E0Q)], incs=[(s_vg, 16)])
            C_DBG_E0 = s_vg.count
        wv = [(s_pe, C_OUT0MM)]
        if debug and C_DBG_E0 > 0:
            wv.append((s_vg, C_DBG_E0))
        P.emit("vector", lambda en: en.memset(U[0:64, :], 0.0), waits=wv)
        P.emit("vector", lambda en: en.memset(U[64:128, :], 0.0))
        P.emit("vector", lambda en: en.tensor_copy(U[0:16, 0:1024], psA[0:16, :]))
        P.emit("vector", lambda en: en.tensor_copy(U[0:16, 1024:1040], psB[0:16, 0:16]),
               incs=[(s_ve, 1)])
        C_OUT0SB = s_ve.count

        P.emit("vector", lambda en: en.memset(E0sp[:], 0.0))
        P.emit("vector", lambda en: en.memset(Odsp[:], 0.0), incs=[(s_ve, 1)])
        C_PREMS = s_ve.count
        for h in range(NH):
            P.emit("gpsimd", lambda en, h=h: en.dma_start(
                out=Odsp[0:1, h * 16:h * 16 + 1], in_=U[h:h + 1, h * 65 + 64: h * 65 + 65]),
                waits=[(s_ve, C_OUT0SB), (s_ve, C_PREMS)] if h == 0 else (), incs=[(s_vg, 16)])
        for h in range(NH):
            er = 32 * (h % 4)
            w = h // 4
            src = bass.AP(E0, er * (NF * H) + w * S + 256, [[NF * H, 1], [1, 1792]])
            dst = bass.AP(E0sp, 16 * 256 + h * 16, [[256, 112], [1, 16]])
            P.emit("gpsimd", lambda en, s=src, d=dst: en.dma_start(out=d, in_=s),
                   waits=[(s_ac, C_E0), (s_ve, C_PREMS)] if h == 0 else (), incs=[(s_vg, 16)])
        C_ESP = s_vg.count
        P.emit("tensor", lambda en: en.matmul(
            psB[:, 0:128], rc2[:, 0:128], rc2[:, 0:128], start=True, stop=False),
            waits=[(s_ve, C_OUT0SB)], incs=[(s_pe, 1)])
        for h in range(NH):
            hr = (h % 2) * 64
            col = (h // 2) * 16 + h
            P.emit("tensor", lambda en, h=h, hr=hr, col=col: en.matmul(
                psB[hr:hr + 64, col:col + 1], U[0:16, h * 65: h * 65 + 64],
                idbf[0:16, h:h + 1], start=False, stop=False,
                tile_position=(0, hr)))
        P.emit("tensor", lambda en: en.matmul(
            psB[:, 128:256], rc2[:, 0:128], rc2[:, 0:128], start=False, stop=True),
            incs=[(s_pe, 1)])
        C_W1C = s_pe.count
        P.emit("vector", lambda en: en.tensor_copy(W1c[:], psB[:, 0:128]),
               waits=[(s_pe, C_W1C)], incs=[(s_ve, 1)])
        C_W1CE = s_ve.count
        for sc in range(2):
            for hc in range(NF):
                wv = []
                if sc == 0 and hc == 0:
                    wv = [(s_vg, C_VGC), (s_ldw, CW_O), (s_ve, C_OUT0SB)]
                P.emit("tensor", lambda en, sc=sc, hc=hc: en.matmul(
                    psA[0:16, sc * 512:(sc + 1) * 512], V0c[:, hc * 16:(hc + 1) * 16],
                    wbfB[:, hc * H + sc * 512: hc * H + (sc + 1) * 512],
                    start=(hc == 0), stop=(hc == NF - 1)),
                    waits=wv, incs=[(s_pe, 1)] if (sc == 1 and hc == NF - 1) else ())
        C_W0MM = s_pe.count
        P.emit("vector", lambda en: en.memset(W01[:, 0:1024], 0.0),
               waits=[(s_vg, C_ESP)])
        P.emit("vector", lambda en: en.tensor_copy(W01[0:16, 0:1024], psA[0:16, 0:1024]),
               waits=[(s_pe, C_W0MM)], incs=[(s_ve, 1)])
        C_W0E = s_ve.count
        for sc in range(2):
            for hc in range(NF):
                wv = []
                if sc == 0 and hc == 0:
                    wv = [(s_ve, C_W0E), (s_ve, C_W1CE)]
                P.emit("tensor", lambda en, sc=sc, hc=hc: en.matmul(
                    psA[0:16, sc * 512:(sc + 1) * 512], W1c[:, hc * 16:(hc + 1) * 16],
                    wbfB[:, hc * H + sc * 512: hc * H + (sc + 1) * 512],
                    start=(hc == 0), stop=(hc == NF - 1)),
                    waits=wv, incs=[(s_pe, 1)] if (sc == 1 and hc == NF - 1) else ())
        C_W1MM = s_pe.count
        P.emit("vector", lambda en: en.tensor_copy(W01[32:48, 0:1024], psA[0:16, 0:1024]),
               waits=[(s_pe, C_W1MM)], incs=[(s_ve, 1)])
        C_W01 = s_ve.count

        ps_free = [C_E0Q, C_E0]
        ebuf_free = [0, 0, 0, 0]
        pb_free = [[], [], [], []]
        pending_scq3 = []
        last_evq2 = [0]
        mul_done = {}
        rc2a_done = {}
        rc2b_done = {}
        chain_in2a = {}
        chain_in2b = {}
        last_gsp = [None]
        pending_tail = []
        recipa_done = {}
        recipb_done = {}
        dspr_free = {}
        rc2_read = {}
        pending_norm = []
        sinfo = {}
        bank_open = set()
        touched = set()
        slots = [psA, psB]
        DEPTH = 3
        NPS = NH * NT // 2

        def emit_mul_job(hp_j, x, q):
            hr2 = x * 64
            dst = attnT[hr2:hr2 + 64, hp_j * S + q * 512: hp_j * S + (q + 1) * 512]
            meng, msem = ("vector", s_ve) if x == 0 else ("gpsimd", s_gp)
            wv2 = [(s_dsc, (rc2b_done if q == 3 else rc2a_done)[(hp_j, x)])]
            if debug and C_DBG_P0 > 0:
                wv2.append((s_vg, C_DBG_P0))
            P.emit(meng, lambda en, d=dst, hr2=hr2, q=q: en.tensor_tensor(
                d, scA[hr2:hr2 + 64, q * 512:(q + 1) * 512],
                rc2[hr2:hr2 + 64, q * 512:(q + 1) * 512], ALU.mult),
                waits=wv2, incs=[(msem, 1)])
            mul_done[(x, q)] = (msem, msem.count)
            rc2_read[x] = (msem, msem.count)

        def emit_g_tail(hp_j, x):
            h = 2 * hp_j + x
            key = (hp_j, x)
            wv_g = [(s_ve, recipb_done[key])]
            if last_gsp[0] is not None:
                wv_g.append(last_gsp[0])
            P.emit("gpsimd", lambda en, h=h, x=x: en.tensor_mul(
                Gsp[:], E0sp[:, h * 16:(h + 1) * 16], Dspr[:, 16 * x:16 * x + 16]),
                waits=wv_g, incs=[(s_gp, 1)])
            c_gm = s_gp.count
            P.emit("gpsimd", lambda en, h=h: en.dma_start(
                out=U[h:h + 1, :], in_=Gsp[:]),
                waits=[(s_gp, c_gm)], incs=[(s_vg, 16)])
            c_gg = s_vg.count
            P.emit("gpsimd", lambda en, h=h, x=x: en.dma_start(
                out=U[32 + h:33 + h, 0:1], in_=Dspr[0:1, 16 * x:16 * x + 1]),
                incs=[(s_vg, 16)])
            last_gsp[0] = (s_vg, c_gg)
            dspr_free[key].extend([(s_gp, c_gm), (s_vg, s_vg.count)])

        def emit_front(pst):
            hp, tc = divmod(pst, NT)
            slot = pst % 2
            eslot = pst % 4
            sl = slots[slot]
            if tc == 0:
                q0, q1, ocol, ncol = 0, 2, 128, 256
            elif tc == NT - 1:
                q0, q1, ocol, ncol = tc - 1, tc + 1, 0, 256
            else:
                q0, q1, ocol, ncol = tc - 1, tc + 2, 0, 384
            wv = [(s_ve, C_W01)]
            if ps_free[slot] > 0:
                wv.append((s_ac, ps_free[slot]))
            for x in range(2):
                hr = x * 64
                lhsT = KT[hr:hr + 64, hp * S + tc * 128: hp * S + (tc + 1) * 128]
                rhs = QT[hr:hr + 64, hp * S + q0 * 128: hp * S + q1 * 128]
                P.emit("tensor", lambda en, sl=sl, l=lhsT, r=rhs, oc=x * 512 + ocol, nn=ncol, hr=hr: en.matmul(
                    sl[:, oc:oc + nn], l, r, start=True, stop=True, tile_position=(hr, 0)),
                    waits=wv if x == 0 else (), incs=[(s_pe, 1)] if x == 1 else ())
            C_SC = s_pe.count
            wv_e = [(s_pe, C_SC)]
            if ebuf_free[eslot] > 0:
                wv_e.append((s_pe, ebuf_free[eslot]))
            P.emit("scalar", lambda en, sl=sl, es=eslot: en.activation(
                Esb[:, es * 896: es * 896 + 896], sl[:, 0:896],
                AF.Exp, scale=float(SCALE)),
                waits=wv_e, incs=[(s_ac, 1)])
            ps_free[slot] = s_ac.count
            C_EXP = s_ac.count
            minfo = []
            for x in range(2):
                base = eslot * 896 + x * 512
                if tc == 0:
                    eap = Esb[:, base + 256: base + 384]
                    map_ = masks[:, 512:640]
                elif tc == 1:
                    eap = Esb[:, base: base + 384].rearrange("p (a b) -> p a b", b=128)[:, 0:3:2, :]
                    map_ = masks[:, 256:512].rearrange("p (a b) -> p a b", b=128)
                elif tc == NT - 1:
                    eap = Esb[:, base: base + 128]
                    map_ = masks[:, 0:128]
                else:
                    eap = Esb[:, base: base + 384].rearrange("p (a b) -> p a b", b=128)[:, 0:3:2, :]
                    map_ = masks[:, 0:256].rearrange("p (a b) -> p a b", b=128)
                eng, sem = (("vector", s_ve) if (pst + x) % 2 == 0 else ("gpsimd", s_gp))
                P.emit(eng, lambda en, eap=eap, map_=map_: en.tensor_tensor(eap, eap, map_, ALU.mult),
                       waits=[(s_ac, C_EXP), (s_ldc, C_CONST)], incs=[(sem, 1)])
                minfo.append((sem, sem.count))
            sinfo[pst] = (hp, tc, eslot, slot, minfo)

        def emit_back(pst):
            hp, tc, eslot, slot, minfo = sinfo.pop(pst)
            if tc == 0:
                blocks = [(128, 0), (256, 1)]
            elif tc == NT - 1:
                blocks = [(0, tc - 1), (128, tc)]
            else:
                blocks = [(0, tc - 1), (128, tc), (256, tc + 1)]
            stop_Tq = (tc - 1) if tc in (4, 8, 12) else (15 if tc == 15 else None)
            for x in range(2):
                h = 2 * hp + x
                lhsT_v = V[:, tc * 1040 + h * 65: tc * 1040 + (h + 1) * 65]
                runs = []
                for ecol, Tq in blocks:
                    q = Tq // 4
                    b = 2 * x + (q % 2)
                    col = (Tq % 4) * 128
                    virgin = (x, Tq) not in touched
                    touched.add((x, Tq))
                    if (runs and runs[-1][1] == b and runs[-1][2] + runs[-1][3] == col
                            and runs[-1][4] == virgin):
                        runs[-1][3] += 128
                        runs[-1][5] = runs[-1][5] or (Tq == stop_Tq)
                    else:
                        runs.append([ecol, b, col, 128, virgin, Tq == stop_Tq])
                first = True
                for ecol, b, col, width, _virgin, stp in runs:
                    fresh = b not in bank_open
                    bank_open.add(b)
                    if stp:
                        bank_open.discard(b)
                        for Tq in range(4 * (stop_Tq // 4), 4 * (stop_Tq // 4) + 4):
                            touched.discard((x, Tq))
                    wv = []
                    if first:
                        wv.append(minfo[x])
                        first = False
                    if fresh:
                        wv.extend(pb_free[b])
                    P.emit("tensor", lambda en, bb=pb[b], l=lhsT_v, es=eslot, x=x, ec=ecol, col=col, wd=width, sf=fresh, sp=stp: en.matmul(
                        bb[0:65, col:col + wd], l,
                        Esb[:, es * 896 + x * 512 + ec: es * 896 + x * 512 + ec + wd],
                        start=sf, stop=sp),
                        waits=wv, incs=[(s_pe, 1)])
            ebuf_free[eslot] = s_pe.count

            if tc in (5, 6, 7, 8):
                for _ in range(2):
                    if pending_norm:
                        hp_j, x_j, q_j = pending_norm.pop(0)
                        emit_mul_job(hp_j, x_j, q_j)

            if tc == 1 and pending_scq3:
                for b_i, xr, wv_d, dn in pending_scq3:
                    P.emit("vector", lambda en, b=pb[b_i], xr=xr: en.tensor_copy(
                        scA[xr:xr + 64, 3 * 512:4 * 512], b[0:64, :]),
                        waits=wv_d, incs=[(s_ve, 1)])
                    pb_free[b_i] = [(s_ve, s_ve.count), dn]
                pending_scq3.clear()

            evq = {6: 0, 8: 1, 12: 2}.get(tc)
            if evq is not None:
                C_AVB = s_pe.count
                for x in range(2):
                    bk = 2 * x + (evq % 2)
                    p0 = 64 + 32 * x
                    wv_e = [(s_pe, C_AVB)]
                    if (x, evq) in mul_done:
                        wv_e.append(mul_done[(x, evq)])
                    if (hp - 1, x) in chain_in2a:
                        wv_e.append((s_vg, chain_in2a[(hp - 1, x)]))
                    P.emit("vector", lambda en, b=pb[bk], xr=x * 64, q=evq: en.tensor_copy(
                        scA[xr:xr + 64, q * 512:(q + 1) * 512], b[0:64, :]),
                        waits=wv_e)
                    P.emit("vector", lambda en, b=pb[bk], p0=p0, q=evq: en.tensor_copy(
                        U[p0:p0 + 1, q * 512:(q + 1) * 512], b[64:65, :]),
                        incs=[(s_ve, 1)])
                    pb_free[bk] = [(s_ve, s_ve.count)]
                if evq == 2:
                    last_evq2[0] = s_ve.count

            if tc in (5, 6) and pending_tail:
                emit_g_tail(*pending_tail.pop(0))

            if tc == 15:
                C_AVB15 = s_pe.count
                dnq3 = {}
                for x in range(2):
                    bk = 2 * x + 1
                    p0 = 64 + 32 * x
                    wv_dn = [(s_pe, C_AVB15)]
                    if (hp - 1, x) in chain_in2b:
                        wv_dn.append((s_vg, chain_in2b[(hp - 1, x)]))
                    P.emit("scalar", lambda en, b=pb[bk], p0=p0: en.copy(
                        U[p0:p0 + 1, 3 * 512:4 * 512], b[64:65, :]),
                        waits=wv_dn, incs=[(s_ac, 1)])
                    dnq3[x] = s_ac.count
                    wv_sc = [(s_pe, C_AVB15)]
                    if (x, 3) in mul_done:
                        wv_sc.append(mul_done[(x, 3)])
                    pending_scq3.append((bk, x * 64, wv_sc, (s_ac, dnq3[x])))
                for x in range(2):
                    h = 2 * hp + x
                    p0 = 64 + 32 * x
                    xs = 16 * x
                    prevkey = (hp - 1, x)
                    row_ap = bass.AP(U, p0 * S, [[S, 1], [1, S]])
                    wv_cp = [(s_vg, C_ESP)]
                    if prevkey in recipb_done:
                        wv_cp.append((s_ve, recipb_done[prevkey]))
                    P.emit("gpsimd", lambda en, h=h, xs=xs: en.tensor_copy(
                        Dsp[:, xs:xs + 16], E0sp[:, h * 16:(h + 1) * 16]),
                        waits=wv_cp)
                    P.emit("gpsimd", lambda en, h=h, xs=xs: en.tensor_add(
                        Dsp[0:1, xs:xs + 16], Dsp[0:1, xs:xs + 16],
                        Odsp[0:1, h * 16:(h + 1) * 16]),
                        incs=[(s_gp, 1)])
                    P.emit("gpsimd", lambda en, ap=row_ap, xs=xs: en.dma_start(
                        out=Dsp[:, xs:xs + 16], in_=ap, accum_op=ALU.add),
                        waits=[(s_ve, last_evq2[0]), (s_ac, dnq3[x])], incs=[(s_vg, 16)])
                    chain_in2b[(hp, x)] = s_vg.count
                    chain_in2a[(hp, x)] = s_vg.count
                    wv_r = [(s_vg, chain_in2b[(hp, x)])]
                    if prevkey in dspr_free:
                        wv_r.extend(dspr_free[prevkey])
                    P.emit("vector", lambda en, xs=xs: en.reciprocal(
                        Dspr[:, xs:xs + 16], Dsp[:, xs:xs + 16]),
                        waits=wv_r, incs=[(s_ve, 1)])
                    recipb_done[(hp, x)] = s_ve.count
                    recipa_done[(hp, x)] = s_ve.count
                    P.emit("sync", lambda en, x=x, xs=xs: en.dma_start(
                        out=dsprd[x:x + 1, :], in_=Dspr[:, xs:xs + 16]),
                        waits=[(s_ve, recipb_done[(hp, x)])], incs=[(s_dsc, 16)])
                    c_td = s_dsc.count
                    wv_f = [(s_dsc, c_td)]
                    if x in rc2_read:
                        wv_f.append(rc2_read[x])
                    rc2_ap = bass.AP(rc2, x * 64 * S, [[S, 64], [1, S]])
                    src_ap = bass.AP(dsprd.tensor, x * S, [[0, 64], [1, S]])
                    P.emit("sync", lambda en, d=rc2_ap, s=src_ap: en.dma_start(out=d, in_=s),
                           waits=wv_f, incs=[(s_dsc, 16)])
                    rc2a_done[(hp, x)] = s_dsc.count
                    rc2b_done[(hp, x)] = s_dsc.count
                    dspr_free[(hp, x)] = [(s_dsc, c_td)]
                    pending_tail.append((hp, x))
                for q in range(4):
                    for x in range(2):
                        pending_norm.append((hp, x, q))

        for i in range(NPS + DEPTH):
            if i < NPS:
                emit_front(i)
            if i >= DEPTH:
                emit_back(i - DEPTH)
        for b_i, xr, wv_d, dn in pending_scq3:
            P.emit("vector", lambda en, b=pb[b_i], xr=xr: en.tensor_copy(
                scA[xr:xr + 64, 3 * 512:4 * 512], b[0:64, :]),
                waits=wv_d, incs=[(s_ve, 1)])
        pending_scq3.clear()
        C_PSFREE_VE = s_ve.count
        C_PSFREE_AC = max(ps_free) if isinstance(ps_free[0], int) else 0
        while pending_tail:
            emit_g_tail(*pending_tail.pop(0))
        while pending_norm:
            hp_j, x_j, q_j = pending_norm.pop(0)
            emit_mul_job(hp_j, x_j, q_j)
        C_ATTN = s_ve.count
        C_ATTN_GP = s_gp.count
        C_GDONE = s_dsc.count
        C_GDONE_VG = s_vg.count
        C_DBG_P2 = 0
        if debug:
            P.emit("gpsimd", lambda en: en.dma_start(out=dbg["d_attnT"], in_=attnT[:]),
                   waits=[(s_ve, C_ATTN), (s_gp, C_ATTN_GP)], incs=[(s_vg, 16)])
            C_DBG_P2 = s_vg.count

        P.emit("sync", lambda en: en.dma_start(out=biasr, in_=biasr_d),
               waits=[(s_ve, C_ATTN), (s_gp, C_ATTN_GP)], incs=[(s_ldc, 16)])
        C_BIAS = s_ldc.count
        fbank_free = {0: 0, 1: 0}
        for f in range(NF):
            banks = [psum_rot[(f % 2) * 4 + j] for j in range(4)]
            for cc in range(NF):
                lhsT = wf8[:, cc * H + f * 128: cc * H + (f + 1) * 128]
                for sc in range(4):
                    wv = []
                    if cc == 0 and sc == 0:
                        wv = [(s_ldw, CW_F), (s_ve, C_PSFREE_VE), (s_ac, C_PSFREE_AC)]
                        if debug and C_DBG_P2 > 0:
                            wv.append((s_vg, C_DBG_P2))
                        if fbank_free[f % 2] > 0:
                            wv.append((s_ac, fbank_free[f % 2]))
                    bt, bb0 = banks[sc]
                    P.emit("tensor", lambda en, b=bt, b0=bb0, l=lhsT, cc=cc, sc=sc: en.matmul(
                        b[:, b0:b0 + 512], l, xqT[:, cc * S + sc * 512: cc * S + (sc + 1) * 512],
                        start=(cc == 0), stop=(cc == NF - 1)),
                        waits=wv,
                        incs=[(s_pe, 1)] if (cc == NF - 1 and sc == 3) else ())
            C_FG = s_pe.count
            for sc in range(4):
                dst = ffn16[:, f * S + sc * 512: f * S + (sc + 1) * 512]
                bt, bb0 = banks[sc]
                P.emit("scalar", lambda en, b=bt, b0=bb0, d=dst: en.copy(d, b[:, b0:b0 + 512]),
                       waits=[(s_pe, C_FG)] if sc == 0 else (), incs=[(s_ac, 1)])
            fbank_free[f % 2] = s_ac.count
        C_FFN = s_ac.count
        if debug:
            P.emit("gpsimd", lambda en: en.dma_start(out=dbg["d_ffn"], in_=ffn16[:, 0:NF * S]),
                   waits=[(s_ac, C_FFN)], incs=[(s_vg, 16)])

        if debug:
            P.emit("gpsimd", lambda en: en.dma_start(out=dbg["d_ffn"], in_=ffn16[:, 0:NF * S]),
                   waits=[(s_ac, C_FFN)], incs=[(s_vg, 16)])

        wo_bank_free = {}
        xre_used = {}
        tok_free = {}
        out_used = {}
        for sc in range(4):
            ot_slab = otbf[:, (sc % 2) * 4096:(sc % 2) * 4096 + 4096]
            xr_cnt = {}
            for i in range(4):
                st_ = sc * 4 + i
                xsl_i = st_ % 4
                wv = [(s_ldc, C_BIAS), (s_ac, C_FFN)]
                if xre_used.get(xsl_i, 0) > 0:
                    wv.append((s_ve, xre_used[xsl_i]))
                P.emit("vector", lambda en, xs=xsl_i: en.tensor_copy(
                    x_re[:, xs * 1024:(xs + 1) * 1024], biasr),
                    waits=wv, incs=[(s_ve, 1)])
                C_PRE = s_ve.count
                P.emit("gpsimd", lambda en, st_=st_, xs=xsl_i: en.dma_start(
                    out=x_re[:, xs * 1024:(xs + 1) * 1024],
                    in_=x_d[st_ * 128:(st_ + 1) * 128, :], accum_op=ALU.add),
                    waits=[(s_ve, C_PRE)], incs=[(s_xr, 16)])
                xr_cnt[i] = s_xr.count
            for wave in range(2):
                for oo in range(4):
                    o = wave * 4 + oo
                    for hc in range(NF):
                        lhsT = wbfB[:, hc * H + o * 128: hc * H + (o + 1) * 128]
                        wv = []
                        if hc == 0:
                            wv = [(s_ldw, CW_O), (s_ve, C_ATTN), (s_gp, C_ATTN_GP), (s_ac, C_FFN)]
                            if wo_bank_free.get(oo, 0) > 0:
                                wv.append((s_ac, wo_bank_free[oo]))
                        P.emit("tensor", lambda en, b=pb[oo], l=lhsT, hc=hc, sc=sc: en.matmul(
                            b[:], l, attnT[:, hc * S + sc * 512: hc * S + (sc + 1) * 512],
                            start=(hc == 0), stop=False),
                            waits=wv,
                            incs=[(s_pe, 1)] if hc == NF - 1 else ())
                    P.emit("tensor", lambda en, o=o, sc=sc, b=pb[oo]: en.matmul(
                        b[:], W01[:, o * 128:(o + 1) * 128], U[:, sc * 512:(sc + 1) * 512],
                        start=False, stop=True),
                        waits=[(s_vg, C_GDONE_VG), (s_dsc, C_GDONE)], incs=[(s_pe, 1)])
                    C_WO = s_pe.count
                    dsl = ot_slab[:, o * 512:(o + 1) * 512]
                    P.emit("scalar", lambda en, b=pb[oo], d=dsl: en.copy(d, b[:]),
                           waits=[(s_pe, C_WO)], incs=[(s_ac, 1)])
                    wo_bank_free[oo] = s_ac.count
            C_OT = s_ac.count

            for i in range(4):
                st_ = sc * 4 + i
                xsl_i = st_ % 4
                C_XR = xr_cnt[i]

                pair = [(psA, 0), (psA, 512)] if i % 2 == 0 else [(psB, 0), (psB, 512)]
                wv_t = [(s_ac, C_OT)]
                if tok_free.get(i % 2, 0) > 0:
                    wv_t.append((s_ve, tok_free[i % 2]))
                for f in range(NF):
                    pt, pt0 = pair[f // 4]
                    P.emit("tensor", lambda en, pt=pt, p0=pt0, f=f, i=i, ot=ot_slab: en.matmul(
                        pt[:, p0 + (f % 4) * 128:p0 + (f % 4 + 1) * 128],
                        ot[:, f * 512 + i * 128: f * 512 + i * 128 + 128],
                        idbf[:], start=True, stop=False),
                        waits=wv_t if f == 0 else ())
                    P.emit("tensor", lambda en, pt=pt, p0=pt0, f=f, sc=sc, i=i: en.matmul(
                        pt[:, p0 + (f % 4) * 128:p0 + (f % 4 + 1) * 128],
                        ffn16[:, f * S + sc * 512 + i * 128: f * S + sc * 512 + i * 128 + 128],
                        idf16[:], start=False, stop=True),
                        incs=[(s_pe, 1)] if f == NF - 1 else ())
                C_TOK = s_pe.count

                oslot = st_ % 2
                wv_f = [(s_pe, C_TOK), (s_xr, C_XR)]
                if out_used.get(oslot, 0) > 0:
                    wv_f.append((s_st, out_used[oslot]))
                for hlf in range(2):
                    pt, pt0 = pair[hlf]
                    P.emit("vector", lambda en, os_=oslot, xs=xsl_i, hh=hlf, pt=pt, p0=pt0: en.tensor_add(
                        out_f32[:, os_ * 1024 + hh * 512: os_ * 1024 + (hh + 1) * 512],
                        x_re[:, xs * 1024 + hh * 512: xs * 1024 + (hh + 1) * 512],
                        pt[:, p0:p0 + 512]),
                        waits=wv_f if hlf == 0 else (), incs=[(s_ve, 1)])
                xre_used[xsl_i] = s_ve.count
                tok_free[i % 2] = s_ve.count
                C_OUT = s_ve.count
                P.emit("sync", lambda en, st_=st_, os_=oslot: en.dma_start(
                    out=y_d[st_ * 128:(st_ + 1) * 128, :],
                    in_=out_f32[:, os_ * 1024:(os_ + 1) * 1024]),
                    waits=[(s_ve, C_OUT)], incs=[(s_st, 16)])
                out_used[oslot] = s_st.count

        if debug or light:
            P.emit("gpsimd", lambda en: en.dma_start(out=dbg["d_u"], in_=U[:]),
                   waits=[(s_st, s_st.count)], incs=[(s_vg, 16)])
            P.emit("gpsimd", lambda en: en.dma_start(out=dbg["d_rc2"], in_=rc2[:]),
                   incs=[(s_vg, 16)])
            if light:
                P.emit("gpsimd", lambda en: en.dma_start(out=dbg["d_attnT"], in_=attnT[:]),
                       incs=[(s_vg, 16)])
            P.emit("sync", lambda en: en.nop(), waits=[(s_vg, s_vg.count)])
        P.emit("sync", lambda en: en.nop(), waits=[(s_st, s_st.count)])
        P.finalize()
    return nc


def _host_inputs(L0_attn_x, L0_attn_Wq, L0_attn_Wk, L0_attn_Wv, L0_attn_Wo,
                 L0_ffn_weight, L0_ffn_bias):
    x = np.ascontiguousarray(np.asarray(L0_attn_x, dtype=np.float32))
    common = dict(
        wqT=np.ascontiguousarray(np.asarray(L0_attn_Wq, np.float32).T),
        wkT=np.ascontiguousarray(np.asarray(L0_attn_Wk, np.float32).T),
        wvT=np.ascontiguousarray(np.asarray(L0_attn_Wv, np.float32).T),
        woT=np.ascontiguousarray(np.asarray(L0_attn_Wo, np.float32).T),
        wfT=np.ascontiguousarray(np.asarray(L0_ffn_weight, np.float32).T),
        biasr=np.ascontiguousarray(np.broadcast_to(
            np.asarray(L0_ffn_bias, np.float32)[None, :], (128, H))),
        masks=_tri_masks(),
        idbf=np.eye(128, dtype=ml_dtypes.bfloat16),
        id8=np.eye(128, dtype=ml_dtypes.float8_e4m3),
        idf16=np.eye(128, dtype=np.float16),
        onesbf=np.ones((128, 128), dtype=ml_dtypes.bfloat16),
    )
    return x, common


def kernel(L0_attn_x, L0_attn_Wq, L0_attn_Wk, L0_attn_Wv, L0_attn_Wo,
           L0_ffn_weight, L0_ffn_bias):
    x, common = _host_inputs(L0_attn_x, L0_attn_Wq, L0_attn_Wk, L0_attn_Wv,
                             L0_attn_Wo, L0_ffn_weight, L0_ffn_bias)
    nc = build()
    in_maps = [dict(x=np.ascontiguousarray(x[b]), **common) for b in range(8)]
    res = run_bass_kernel_spmd(nc, in_maps, core_ids=list(range(8)))
    return np.stack([r["y"] for r in res.results], axis=0)
